# version 07
# baseline (speedup 1.0000x reference)
import numpy as np
import ml_dtypes
import concourse.bass as bass
import concourse.mybir as mybir
from concourse.bass_utils import run_bass_kernel_spmd

F32 = mybir.dt.float32
BF16 = mybir.dt.bfloat16
I32 = mybir.dt.int32
AF = mybir.ActivationFunctionType
ALU = mybir.AluOpType
AX = mybir.AxisListType.X

B, S, D, H = 2, 8192, 1024, 16
HD, HL = 64, 4
NT = 64
NTB = 16
NSEL = 48
SK = 2048
NKT = SK // 128
QW = 259

DEBUG = False
DBGSEL = 0x000000


class Sem:
    __slots__ = ("h", "n", "name")

    def __init__(self, h, name):
        self.h = h
        self.n = 0
        self.name = name


class Ctx:
    def __init__(self, eng, me, record=None):
        self.eng = eng
        self.me = me
        self.record = record
        self.srcline = None

    def do(self, engine, fn, inc=None, inck=1, waits=()):
        if self.record is not None:
            if self.me == engine:
                import traceback
                fr = traceback.extract_stack(limit=3)[0]
                self.record.append(([(s.name, v) for (s, v) in waits if v > 0],
                                    [(inc.name, inck)] if inc is not None else [],
                                    fr.lineno))
        elif self.me == engine:
            for (s, v) in waits:
                if v > 0:
                    self.eng.wait_ge(s.h, v)
            r = fn(self.eng)
            if inc is not None:
                if r is None:
                    raise RuntimeError("op returned None but inc requested")
                r.then_inc(inc.h, inck)
        if inc is not None:
            inc.n += inck

    def barrier(self, sem):
        if self.record is not None:
            sem.n += 5
            self.record.append(([], [(sem.name, 1)], 0))
            self.record.append(([(sem.name, sem.n)], [], 0))
            return
        if self.me == "sync":
            self.eng.sem_inc(sem.h, 1)
        elif self.me == "vector":
            self.eng.tensor_copy(self.bar_buf[0:1, 0:1], self.bar_buf[0:1, 1:2]).then_inc(sem.h, 1)
        elif self.me == "scalar":
            self.eng.copy(out=self.bar_buf[0:1, 2:3], in_=self.bar_buf[0:1, 3:4]).then_inc(sem.h, 1)
        elif self.me == "gpsimd":
            self.eng.memset(self.bar_buf[0:1, 4:5], 0.0).then_inc(sem.h, 1)
        elif self.me == "tensor":
            self.eng.matmul(self.bar_ps, self.bar_w, self.bar_w,
                            start=True, stop=True).then_inc(sem.h, 1)
        sem.n += 5
        self.eng.wait_ge(sem.h, sem.n)


def ts(i, n):
    return slice(i * n, (i + 1) * n)


class Alloc:
    def __init__(self, nc, base, top, tag):
        self.nc, self.base, self.top, self.tag = nc, base, top, tag
        self.cur = (base + 31) // 32 * 32

    def t(self, name, shape, dt):
        per = int(np.prod(shape[1:])) * mybir.dt.size(dt)
        per = (per + 31) // 32 * 32
        h = self.nc.alloc_sbuf_tensor_at(name, shape, dt, offset=self.cur)
        self.cur += per
        assert self.cur <= self.top, f"sbuf overflow [{self.tag}] at {name}: {self.cur} > {self.top}"
        return h


def build_nc(deadlock_check=False):
    nc = bass.Bass("TRN2", target_bir_lowering=False)

    I = lambda n, s, d=F32: nc.dram_tensor(n, s, d, kind="ExternalInput").ap()
    xb = I("xb", [S, D])
    xo = I("xo", [2048, D])
    wvq = I("wvq", [D, 512]); wk = I("wk", [D, 256])
    wo = I("wo", [D, D])
    bk_c = I("bk_c", [128, 2]); bk_r = I("bk_r", [1, 256])
    bo_r = I("bo_r", [1, D])
    bvq_rbD = I("bvq_rbD", [1, 512], BF16)
    sidx = I("sidx", [128, 16], I32)
    cw1 = I("cw1", [D, D]); cw2 = I("cw2", [D, 512]); cw3 = I("cw3", [512, 256])
    cw4_c = I("cw4_c", [128, 2])
    cb1_c = I("cb1_c", [128, 8]); cb2_c = I("cb2_c", [128, 4])
    cb3_c = I("cb3_c", [128, 2]); cb4_s = I("cb4_s", [1, 1])
    gw1 = I("gw1", [D, 512]); gw2 = I("gw2", [512, 256]); gw3_c = I("gw3_c", [128, 2])
    gb1_r = I("gb1_r", [1, 512]); gb2_r = I("gb2_r", [1, 256]); gb3_s = I("gb3_s", [1, 1])
    arng = I("arng", [1, 1])
    tok0 = I("tok0", [1, 1])
    ident_f = I("ident_f", [128, 128])
    ident_b = I("ident_b", [128, 128], BF16)
    iota2k = I("iota2k", [128, SK])
    iota128r = I("iota128r", [128, 128])
    jconst = I("jconst", [128, NSEL])
    i128c_bD = I("i128c_bD", [128, 1], BF16)
    hi64_bD = I("hi64_bD", [128, NT], BF16)
    onesf_D = I("onesf_D", [128, 128])
    onesb_D = I("onesb_D", [128, 128], BF16)

    out_own = nc.dram_tensor("out_own", [2048, D], F32, kind="ExternalOutput").ap()

    cc1_in = nc.dram_tensor("cc1_in", [128, 200], BF16)
    cc1_out = nc.dram_tensor("cc1_out", [512, 200], BF16)
    cc2_in = nc.dram_tensor("cc2_in", [NSEL, HL], F32)
    cc2_out = nc.dram_tensor("cc2_out", [4 * NSEL, HL], F32)
    cc0_in = nc.dram_tensor("cc0_in", [128, 32], F32)
    cc0_out = nc.dram_tensor("cc0_out", [512, 32], F32)
    cc3_in = nc.dram_tensor("cc3_in", [1, D], F32)
    row_bounce = nc.dram_tensor("row_bounce", [1, SK], F32)
    cand_bounce = nc.dram_tensor("cand_bounce", [128, 4 * 8], F32)
    cc3_out = nc.dram_tensor("cc3_out", [4, D], F32)

    dbg = {}
    if DEBUG:
        for nm, shape in [("d_M", [128, 256]), ("d_M2", [128, 256]), ("d_qid", [48, HL]),
                          ("d_kt", [128, 128]), ("d_qt", [128, 259]),
                          ("d_k0m", [128, 256]), ("d_mmax", [128, 256]),
                          ("d_cand", [4, 1024]), ("d_tops", [4, 48]),
                          ("d_rank", [128, 64]), ("d_maskM", [128, 64]),
                          ("d_payall", [128, 800]), ("d_wob", [64, 1024]), ("d_corr", [128, 1024]),
                          ("d_esc", [128, 128]), ("d_qsh", [128, 8]), ("d_corr6", [128, 1024]),
                          ("d_et", [128, 48]), ("d_extr", [48, 3]),
                          ("d_mmean", [128, 256]), ("d_mask", [128, 256]),
                          ("d_pay", [128, 200]), ("d_kbar", [1, 256]),
                          ("d_cnt", [1, SK]), ("d_thr", [1, HL]),
                          ("d_x2", [128, D]), ("d_gf", [1, 1]),
                          ("d_zcs", [1, D]), ("d_gcq", [1, D]), ("d_tf1", [128, NTB]),
                          ("d_grow1", [1, 512]), ("d_grow2", [1, 256]),
                          ("d_corr6b", [128, D]), ("d_qtz", [128, 128]),
                          ("d_rb", [1, D]), ("d_state", [128, D])]:
            dbg[nm] = nc.dram_tensor(nm, shape, F32, kind="ExternalOutput").ap()

    perm = Alloc(nc, nc.sbuf_base, nc.sbuf_top, "perm")
    id_f = perm.t("id_f", [128, 128], F32)
    id_b = perm.t("id_b", [128, 128], BF16)
    onesf = perm.t("onesf", [128, 128], F32)
    onesb = perm.t("onesb", [128, 128], BF16)
    iota128r_s = perm.t("iota128r_s", [128, 128], F32)
    jconst_s = perm.t("jconst_s", [128, NSEL], F32)
    one1f = perm.t("one1f", [1, 1], F32)
    vfb = perm.t("vfb", [128, 16], F32)
    barbuf = perm.t("barbuf", [1, 8], F32)
    zrow = perm.t("zrow", [1, 128], F32)
    payload = perm.t("payload", [128, 200], BF16)
    pay_all = perm.t("pay_all", [128, 800], BF16)
    qid_c = perm.t("qid_c", [48, HL], F32)
    qid_all = perm.t("qid_all", [16, NSEL], F32)
    tok0_s = perm.t("tok0_s", [1, 1], F32)
    tok0b = perm.t("tok0b", [128, 1], F32)
    arng_s = perm.t("arng_s", [1, 1], F32)
    constv = {}
    for val in (1e-5, 0.125, 1.0 / S):
        t_ = perm.t(f"cst_{len(constv)}", [128, 1], F32)
        nc.const_aps.register(F32, val, t_[:])
        constv[val] = t_
    pstart = perm.cur

    A = Alloc(nc, pstart, nc.sbuf_top, "A")
    wvq_s = A.t("wvq_s", [128, 8 * 512], BF16)
    wk_s = A.t("wk_s", [128, 2048], BF16)
    bk_cs = A.t("bk_cs", [128, 2], F32)
    bk_rs = A.t("bk_rs", [1, 256], F32)
    bvq_rb = A.t("bvq_rb", [1, 512], BF16)
    i128c_b = A.t("i128c_b", [128, 1], BF16)
    hi64_b = A.t("hi64_b", [128, NT], BF16)
    iota2k_s = A.t("iota2k_s", [128, SK], F32)
    xsum = A.t("xsum", [128, NT], F32)
    xsq = A.t("xsq", [128, NT], F32)
    xsum_l = A.t("xsum_l", [128, 16], F32)
    xsq_l = A.t("xsq_l", [128, 16], F32)
    rstd = A.t("rstd", [128, NT], F32)
    mrs = A.t("mrs", [128, NT], F32)
    xt = [A.t(f"xt{i}", [128, D], F32) for i in range(2)]
    yt = [A.t(f"yt{i}", [128, D], BF16) for i in range(2)]
    yT4 = [A.t(f"yT4{i}", [128, 4096], BF16) for i in range(2)]
    kT = A.t("kT", [128, 2 * S], BF16)
    k0m = A.t("k0m", [128, 2 * SK], BF16)
    vtok = A.t("vtok", [128, HL * NT * 65], BF16)
    qtok = A.t("qtok", [128, NT * QW], BF16)
    sidx_s = A.t("sidx_s", [128, 16], I32)
    sidx_f = A.t("sidx_f", [128, 16], F32)
    etile = A.t("etile", [128, SK], BF16)
    cnt_row = A.t("cnt_row", [1, SK], F32)
    cnt_col_b = A.t("cnt_col_b", [128, 16], BF16)
    mask_row = A.t("mask_row", [1, SK], F32)
    maskb = A.t("maskb", [128, SK], BF16)
    ybw_row = A.t("ybw_row", [1, D], F32)
    ybw_col_b = A.t("ybw_col_b", [128, 8], BF16)
    kbar_row = A.t("kbar_row", [1, 256], F32)
    kbar_col = A.t("kbar_col", [128, 2], F32)
    kbar_col_b = A.t("kbar_col_b", [128, 2], BF16)
    qTz = [A.t(f"qTz{i}", [128, 128], BF16) for i in range(4)]
    mst = A.t("mst", [128, 4], F32)
    mmax = A.t("mmax", [128, HL * NT], F32)
    mmean = A.t("mmean", [128, HL * NT], F32)
    mmat = A.t("mmat", [128, HL * NT], F32)
    mwork = A.t("mwork", [128, HL * NT], F32)
    cand = A.t("cand", [128, 8], F32)
    candf4 = A.t("candf4", [HL, 1024], F32)
    g84 = A.t("g84", [HL, 8], F32)
    d4 = A.t("d4", [HL, HL], F32)
    topsq = A.t("topsq", [HL, NSEL], F32)
    thr_c = A.t("thr_c", [128, HL], F32)
    maskM = A.t("maskM", [128, NT], F32)
    zer64 = A.t("zer64", [128, NT], F32)
    rankM = A.t("rankM", [128, NT], F32)
    rtot = A.t("rtot", [1, 128], F32)
    rsc = A.t("rsc", [1, 128], F32)
    roff_c = A.t("roff_c", [128, 1], F32)
    et_s = [A.t(f"et_s{i}", [128, NSEL], BF16) for i in range(2)]
    qsel_r = A.t("qsel_r", [48, 64], F32)
    qselT = A.t("qselT", [128, NSEL], BF16)
    qsel_st = A.t("qsel_st", [64, NSEL], BF16)
    extr = A.t("extr", [48, 3], F32)
    pcol = A.t("pcol", [48, 1], F32)
    padrow = A.t("padrow", [1, NSEL], F32)
    m2acc = A.t("m2acc", [128, NT], F32)
    m2fin = A.t("m2fin", [1, 1], F32)
    m2b = A.t("m2b", [128, 1], F32)
    expT = [A.t(f"expT{i}", [128, NSEL + 1], BF16) for i in range(2)]
    ctxTs = A.t("ctxTs", [128, NSEL + 1], F32)
    recipr = A.t("recipr", [128, NSEL], F32)
    recb = A.t("recb", [64, NSEL], F32)
    padb = A.t("padb", [64, NSEL], F32)
    vm_c = A.t("vm_c", [64, 1], F32)
    t48a = A.t("t48a", [64, NSEL], F32)
    t48b = A.t("t48b", [64, NSEL], F32)
    a_end = A.cur

    Bv = Alloc(nc, pstart, nc.sbuf_top, "B1")
    x2 = Bv.t("x2", [128, NTB * D], F32)
    state = Bv.t("state", [128, NTB * D], F32)
    bo_s = Bv.t("bo_s", [1, D], F32)
    rb_row = Bv.t("rb_row", [1, D], F32)
    rb_row_b = Bv.t("rb_row_b", [1, D], BF16)
    qidcol8 = Bv.t("qidcol8", [128, 8], F32)
    qsh8 = Bv.t("qsh8", [128, 8], F32)
    x2sum = Bv.t("x2sum", [128, NTB], F32)
    x2sq = Bv.t("x2sq", [128, NTB], F32)
    rstd2 = Bv.t("rstd2", [128, NTB], F32)
    mrs2 = Bv.t("mrs2", [128, NTB], F32)
    gf_s = Bv.t("gf_s", [1, 1], F32)
    tmp1s = Bv.t("tmp1s", [1, 1], F32)
    cA = Bv.t("cA", [1, 1], F32)
    cB = Bv.t("cB", [1, 1], F32)
    cAb = Bv.t("cAb", [128, 1], F32)
    cBb = Bv.t("cBb", [128, 1], F32)
    gb1_rs = Bv.t("gb1_rs", [1, 512], F32)
    gb2_rs = Bv.t("gb2_rs", [1, 256], F32)
    gb3_ss = Bv.t("gb3_ss", [1, 1], F32)
    gw3_cs = Bv.t("gw3_cs", [128, 2], BF16)
    cw4_cs = Bv.t("cw4_cs", [128, 2], BF16)
    cb1_cs = Bv.t("cb1_cs", [128, 8], F32)
    cb2_cs = Bv.t("cb2_cs", [128, 4], F32)
    cb3_cs = Bv.t("cb3_cs", [128, 2], F32)
    cb4_ss = Bv.t("cb4_ss", [1, 1], F32)
    tfq = Bv.t("tfq", [1, 512], F32)
    tfcol = Bv.t("tfcol", [128, NTB], F32)
    ccol = Bv.t("ccol", [128, NTB], F32)
    nccol = Bv.t("nccol", [128, NTB], F32)
    tmpq = Bv.t("tmpq", [128, 256], F32)
    b2start = Bv.cur
    wo_b = Bv.t("wo_b", [64, 8 * D], BF16)
    corr = Bv.t("corr", [128, 8 * D], BF16)
    xtb = [Bv.t(f"xtb{i}", [128, D], F32) for i in range(2)]
    esc = [Bv.t(f"esc{i}", [128, 128], BF16) for i in range(2)]
    corr_st = Bv.t("corr_st", [48, 8 * D], BF16)
    C2 = Alloc(nc, b2start, nc.sbuf_top, "C2")
    gw1_s = C2.t("gw1_s", [128, 8 * 512], BF16)
    gw2_s = C2.t("gw2_s", [128, 4 * 256], BF16)
    zcs = C2.t("zcs", [1, D], F32)
    gcrow = C2.t("gcrow", [4, D], F32)
    gcq = C2.t("gcq", [1, D], F32)
    gc_col = C2.t("gc_col", [128, 8], BF16)
    grow1 = C2.t("grow1", [1, 512], F32)
    gcol1 = C2.t("gcol1", [128, 4], BF16)
    grow2 = C2.t("grow2", [1, 256], F32)
    gcol2 = C2.t("gcol2", [128, 2], BF16)
    Cv = Alloc(nc, b2start, nc.sbuf_top, "C")
    cw1_s = Cv.t("cw1_s", [128, 8 * D], BF16)
    cw2_s = Cv.t("cw2_s", [128, 8 * 512], BF16)
    cw3_s = Cv.t("cw3_s", [128, 4 * 256], BF16)
    sbuf_b = Cv.t("sbuf_b", [128, D], BF16)
    zb = Cv.t("zb", [128, D], BF16)
    sT = Cv.t("sT", [128, 4096], BF16)
    h1T = Cv.t("h1T", [128, 8 * 512], BF16)
    h2T = Cv.t("h2T", [128, 4 * 512], BF16)
    h3T = Cv.t("h3T", [128, 2 * 512], BF16)
    fin = Cv.t("fin", [128, D], F32)
    b_end = max(Bv.cur, Cv.cur)

    from contextlib import ExitStack
    es = ExitStack()
    PS_A = es.enter_context(nc.psum_tensor("PS_A", [128, 2048], F32))
    PS_B = es.enter_context(nc.psum_tensor("PS_B", [128, 512], F32))
    PS_C = es.enter_context(nc.psum_tensor("PS_C", [128, 512], F32))
    PS_T = es.enter_context(nc.psum_tensor("PS_T", [128, 1024], BF16))
    PS_T2 = es.enter_context(nc.psum_tensor("PS_T2", [128, 1024], BF16))

    sems = {}
    for name in ["s_x", "s_x2", "s_yt", "s_mm", "s_ev", "s_v", "s_s", "s_g",
                 "s_cc", "s_st", "s_ld", "s_tp"] + [f"b{i}" for i in range(10)]:
        sems[name] = Sem(es.enter_context(nc.semaphore(name)), name)

    def prog(c: Ctx):
        s_x, s_yt, s_tp = sems["s_x"], sems["s_yt"], sems["s_tp"]
        s_x2 = sems["s_x2"]
        s_mm, s_ev, s_v = sems["s_mm"], sems["s_ev"], sems["s_v"]
        s_s, s_g, s_cc = sems["s_s"], sems["s_g"], sems["s_cc"]
        s_st, s_ld = sems["s_st"], sems["s_ld"]
        SY, SC, VE, TE, GP = "sync", "scalar", "vector", "tensor", "gpsimd"

        def vfence():
            c.do(VE, lambda e: e.tensor_scalar_add(vfb[:], vfb[:], 0.0))
            c.do(VE, lambda e: e.tensor_scalar_add(vfb[:], vfb[:], 0.0))

        def row2col(dst_ap, src_ap, n, p=128, waits=()):
            c.do(GP, lambda e: e.dma_start(out=row_bounce.ap()[0:1, 0:n], in_=src_ap),
                 inc=s_g, inck=16, waits=list(waits))
            c.do(GP, lambda e: e.dma_start(
                out=dst_ap, in_=row_bounce.ap()[0:1, 0:n].rearrange("o (t j) -> (o j) t", j=p),
                allow_slow_non_contiguous=True),
                inc=s_g, inck=16, waits=[(s_g, s_g.n)])

        for dst, src in [(wvq_s, wvq), (wk_s, wk)]:
            c.do(GP, lambda e, dst=dst, src=src: e.dma_start(
                out=dst[:], in_=src.rearrange("(c p) j -> p c j", p=128)),
                inc=s_g, inck=16)
        for dst, src in [(bk_cs, bk_c), (bk_rs, bk_r), (bvq_rb, bvq_rbD),
                         (id_f, ident_f), (id_b, ident_b),
                         (onesf, onesf_D), (onesb, onesb_D), (iota2k_s, iota2k),
                         (iota128r_s, iota128r), (jconst_s, jconst),
                         (i128c_b, i128c_bD), (hi64_b, hi64_bD), (sidx_s, sidx),
                         (tok0_s, tok0), (arng_s, arng)]:
            c.do(SY, lambda e, dst=dst, src=src: e.dma_start(out=dst[:], in_=src),
                 inc=s_ld, inck=16)
        nload = s_ld.n
        ngp0 = s_g.n
        for val, t_ in constv.items():
            c.do(VE, lambda e, t_=t_, val=val: e.memset(t_[:], float(val)))
        c.do(VE, lambda e: e.memset(one1f[:], 1.0))
        c.do(VE, lambda e: e.memset(vfb[:], 0.0))
        c.do(VE, lambda e: e.memset(zrow[:], 0.0))
        c.do(VE, lambda e: e.memset(zer64[:], 0.0), inc=s_v)
        c.do(VE, lambda e: e.tensor_copy(sidx_f[:], sidx_s[:]), inc=s_v, waits=[(s_ld, nload)])

        for t in range(16):
            buf = t % 2
            eng_q, sem_q = (SY, s_x) if t % 2 == 0 else (SC, s_x2)
            c.do(eng_q, lambda e, t=t, buf=buf: e.dma_start(out=xt[buf][:], in_=xo[ts(t, 128), :]),
                 inc=sem_q, inck=16,
                 waits=([(s_v, s_v.n), (s_s, s_s.n)] if t >= 2 else []))
            c.do(VE, lambda e, t=t, buf=buf: e.reduce_sum(xsum_l[:, t:t + 1], xt[buf][:], axis=AX),
                 inc=s_v, waits=[(sem_q, sem_q.n)])
            c.do(SC, lambda e, t=t, buf=buf: e.activation(out=yt[buf][:], in_=xt[buf][:], func=AF.Square,
                                                          accum_out=xsq_l[:, t:t + 1]),
                 inc=s_s, waits=[(sem_q, sem_q.n)])
        vfence()
        c.do(VE, lambda e: e.tensor_copy(xt[1][:, 0:16], xsum_l[:]), inc=s_v)
        c.do(SC, lambda e: e.copy(out=barbuf[0:1, 6:7], in_=barbuf[0:1, 7:8]))
        c.do(SC, lambda e: e.copy(out=xt[0][:, 0:16], in_=xsq_l[:]), inc=s_s)
        c.do(GP, lambda e: e.dma_start(out=cc0_in.ap()[:, 0:16], in_=xsum_l[:]), inc=s_g, inck=16,
             waits=[(s_v, s_v.n)])
        c.do(GP, lambda e: e.dma_start(out=cc0_in.ap()[:, 16:32], in_=xsq_l[:]), inc=s_g, inck=16,
             waits=[(s_s, s_s.n)])
        c.do(GP, lambda e: e.collective_compute(
            "AllGather", ALU.bypass, replica_groups=[[0, 1, 2, 3], [4, 5, 6, 7]],
            ins=[cc0_in.ap().opt()], outs=[cc0_out.ap().opt()]), inc=s_cc,
            waits=[(s_g, s_g.n)])
        cc0R = cc0_out.ap().rearrange("(r p) w -> p r w", p=128)
        c.do(GP, lambda e: e.dma_start(out=xsum[:].rearrange("p (r t) -> p r t", r=4),
                                       in_=cc0R[:, :, 0:16]), inc=s_g, inck=16,
             waits=[(s_cc, s_cc.n)])
        c.do(GP, lambda e: e.dma_start(out=xsq[:].rearrange("p (r t) -> p r t", r=4),
                                       in_=cc0R[:, :, 16:32]), inc=s_g, inck=16)
        nstat = s_g.n

        for t in range(NKT):
            c.do(VE, lambda e, t=t: e.tensor_single_scalar(etile[:], iota2k_s[:], sidx_f[:, t:t + 1], ALU.is_equal),
                 inc=s_v, waits=([(s_mm, s_mm.n)] if t >= 1 else []))
            for kk in range(4):
                c.do(TE, lambda e, t=t, kk=kk: e.matmul(PS_A[0:1, ts(kk, 512)], onesb[:, 0:1], etile[:, ts(kk, 512)],
                                                        start=(t == 0), stop=(t == NKT - 1)),
                     inc=(s_mm if kk == 3 else None),
                     waits=([(s_v, s_v.n), (s_ld, nload)] if kk == 0 else []))
        c.do(VE, lambda e: e.tensor_copy(cnt_row[:], PS_A[0:1, :]), waits=[(s_mm, s_mm.n)])
        c.do(VE, lambda e: e.tensor_scalar(mask_row[:], cnt_row[:], 0.5, None, ALU.is_gt), inc=s_v)
        row2col(cnt_col_b[:], cnt_row[:], SK, waits=[(s_v, s_v.n)])
        for kk in range(4):
            c.do(TE, lambda e, kk=kk: e.matmul(PS_A[:, ts(kk, 512)], onesf[0:1, :], mask_row[0:1, ts(kk, 512)],
                                               start=True, stop=True),
                 inc=(s_mm if kk == 3 else None), waits=([(s_v, s_v.n)] if kk == 0 else []))
        c.do(VE, lambda e: e.tensor_copy(maskb[:], PS_A[:, 0:SK]), inc=s_v, waits=[(s_mm, s_mm.n)])

        c.do(VE, lambda e: e.tensor_scalar(rstd[:], xsum[:], 1.0 / D, None, ALU.mult),
             inc=s_v, waits=[(s_g, nstat)])
        c.do(VE, lambda e: e.tensor_tensor(mrs[:], rstd[:], rstd[:], ALU.mult), inc=s_v,
             waits=[(s_v, s_v.n)])
        c.do(VE, lambda e: e.scalar_tensor_tensor(mrs[:], xsq[:], 1.0 / D, mrs[:], ALU.mult, ALU.subtract),
             inc=s_v, waits=[(s_v, s_v.n)])
        c.do(SC, lambda e: e.activation(out=mrs[:], in_=mrs[:], func=AF.Sqrt, bias=1e-5, scale=1.0),
             inc=s_s, waits=[(s_v, s_v.n)])
        c.do(VE, lambda e: e.tensor_scalar(xsum[:], xsum[:], 1.0 / D, None, ALU.mult),
             inc=s_v, waits=[(s_s, s_s.n)])
        c.do(VE, lambda e: e.reciprocal(rstd[:], mrs[:]), inc=s_v, waits=[(s_v, s_v.n)])
        c.do(VE, lambda e: e.tensor_tensor(mrs[:], xsum[:], rstd[:], ALU.mult), inc=s_v,
             waits=[(s_v, s_v.n)])
        c.do(VE, lambda e: e.tensor_scalar(mrs[:], mrs[:], -1.0, None, ALU.mult), inc=s_v,
             waits=[(s_v, s_v.n)])

        c.barrier(sems["b0"])

        ev_base = s_ev.n
        evn_t = {}
        evn_g = {}
        for g in range(NT // 4):
            gbuf = g % 2
            for ti in range(4):
                t = g * 4 + ti
                buf = t % 2
                pst = PS_T if t % 2 == 0 else PS_T2
                c.do(SY, lambda e, t=t, buf=buf: e.dma_start(out=xt[buf][:], in_=xb[ts(t, 128), :]),
                     inc=s_x, inck=16, waits=([(s_s, max(0, s_s.n - 1))] if t >= 2 else []))
                c.do(VE, lambda e, t=t, buf=buf: e.tensor_scalar(yt[buf][:], xt[buf][:],
                                                                 rstd[:, t:t + 1], mrs[:, t:t + 1],
                                                                 ALU.mult, ALU.add),
                     inc=s_s, waits=[(s_x, s_x.n)] + ([(s_tp, s_tp.n)] if t >= 2 else []))
                for cc in range(8):
                    c.do(TE, lambda e, cc=cc, buf=buf, pst=pst: e.transpose(
                        pst[:, ts(cc, 128)], yt[buf][:, ts(cc, 128)], id_b[:]),
                        inc=(s_tp if cc == 7 else None),
                        waits=([(s_s, s_s.n), (s_yt, max(0, s_yt.n - 1))] if cc == 0 else []))
                if t < NKT:
                    for half, pq in ((0, PS_A[0:1, 0:512]), (1, PS_A[0:1, 512:1024])):
                        c.do(TE, lambda e, t=t, buf=buf, half=half, pq=pq: e.matmul(
                            pq, cnt_col_b[:, t:t + 1], yt[buf][:, ts(half, 512)],
                            start=(t == 0), stop=(t == NKT - 1)),
                            inc=(s_mm if (t == NKT - 1 and half == 1) else None),
                            waits=([(s_g, s_g.n)] if (t == 0 and half == 0) else []))
                for cc in range(8):
                    c.do(VE, lambda e, cc=cc, g=g, ti=ti, gbuf=gbuf, pst=pst: e.tensor_copy(
                        yT4[gbuf][:, cc * 512 + ti * 128: cc * 512 + (ti + 1) * 128],
                        pst[:, ts(cc, 128)]),
                        inc=(s_yt if cc == 7 else None),
                        waits=([(s_tp, s_tp.n)] if cc == 0 else []))
            wy = [(s_yt, s_yt.n), (s_g, ngp0)]
            for o, pq in ((0, PS_B), (1, PS_C)):
                for cc in range(8):
                    c.do(TE, lambda e, o=o, cc=cc, gbuf=gbuf, pq=pq: e.matmul(
                        pq[:, 0:512], wk_s[:, cc * 256 + o * 128: cc * 256 + (o + 1) * 128],
                        yT4[gbuf][:, ts(cc, 512)], start=(cc == 0), stop=(cc == 7)),
                        inc=(s_mm if cc == 7 else None),
                        waits=((wy + [(s_ev, evn_g.get(g - 1, 0))]) if cc == 0 else []))
            c.do(SC, lambda e, g=g: e.activation(out=kT[:, ts(g, 512)], in_=PS_B[:, 0:512],
                                                 func=AF.Identity, bias=bk_cs[:, 0:1], scale=1.0),
                 waits=[(s_mm, s_mm.n)])
            c.do(SC, lambda e, g=g: e.activation(out=kT[:, S + g * 512: S + (g + 1) * 512],
                                                 in_=PS_C[:, 0:512],
                                                 func=AF.Identity, bias=bk_cs[:, 1:2], scale=1.0),
                 inc=s_ev)
            evn_g[g] = s_ev.n
            for ti in range(4):
                t = g * 4 + ti
                pb = PS_A[:, 1024:1536] if t % 2 == 0 else PS_A[:, 1536:2048]
                for cc in range(8):
                    c.do(TE, lambda e, cc=cc, gbuf=gbuf, ti=ti, pb=pb: e.matmul(
                        pb, yT4[gbuf][:, cc * 512 + ti * 128: cc * 512 + (ti + 1) * 128],
                        wvq_s[:, ts(cc, 512)], start=(cc == 0), stop=False),
                        waits=([(s_ev, evn_t.get(t - 2, 0))] if cc == 0 else []))
                c.do(TE, lambda e, pb=pb: e.matmul(pb, onesb[0:1, :], bvq_rb[:],
                                                   start=False, stop=True), inc=s_mm)
                c.do(SC, lambda e, t=t, pb=pb: e.copy(
                    out=vtok[:].rearrange("p (h t w) -> p h t w", h=HL, t=NT)[:, :, t, 0:64],
                    in_=pb[:, 0:256].rearrange("p (h d) -> p h d", h=HL)),
                    waits=[(s_mm, s_mm.n)])
                c.do(SC, lambda e, t=t, pb=pb: e.copy(out=qtok[:, t * QW: t * QW + 256],
                                                      in_=pb[:, 256:512]),
                     inc=s_ev)
                evn_t[t] = s_ev.n
        qv = qtok[:].rearrange("p (t w) -> p t w", w=QW)
        c.do(VE, lambda e: e.tensor_copy(qv[:, :, 256:257], onesb[:, 0:1].to_broadcast([128, NT, 1])),
             waits=[(s_ev, s_ev.n)])
        c.do(VE, lambda e: e.tensor_copy(qv[:, :, 257:258], i128c_b[:, 0:1].to_broadcast([128, NT, 1])))
        c.do(VE, lambda e: e.tensor_copy(qv[:, :, 258:259], hi64_b[:].rearrange('p t -> p t ()')), inc=s_v)
        vvv = vtok[:].rearrange("p (u w) -> p u w", w=65)
        c.do(VE, lambda e: e.tensor_copy(vvv[:, :, 64:65], onesb[:, 0:1].to_broadcast([128, HL * NT, 1])), inc=s_v)

        c.do(VE, lambda e: e.tensor_copy(ybw_row[:, 0:512], PS_A[0:1, 0:512]), waits=[(s_mm, s_mm.n)])
        c.do(VE, lambda e: e.tensor_copy(ybw_row[:, 512:1024], PS_A[0:1, 512:1024]), inc=s_v)
        row2col(ybw_col_b[:], ybw_row[:], D, waits=[(s_v, s_v.n), (s_yt, s_yt.n)])
        for cc in range(8):
            c.do(TE, lambda e, cc=cc: e.matmul(PS_B[0:1, 0:256], ybw_col_b[:, cc:cc + 1], wk_s[:, ts(cc, 256)],
                                               start=(cc == 0), stop=(cc == 7)),
                 inc=(s_mm if cc == 7 else None),
                 waits=([(s_g, s_g.n), (s_s, s_s.n)] if cc == 0 else []))
        c.do(VE, lambda e: e.scalar_tensor_tensor(kbar_row[:], PS_B[0:1, 0:256], 1.0 / SK, bk_rs[:],
                                                  ALU.mult, ALU.add), inc=s_v, waits=[(s_mm, s_mm.n)])
        vfence()
        c.do(VE, lambda e: e.tensor_copy(kbar_row[:], kbar_row[:]), inc=s_v,
             waits=[(s_v, s_v.n)])
        row2col(kbar_col[:], kbar_row[:], 256, waits=[(s_v, s_v.n)])
        c.do(VE, lambda e: e.tensor_copy(kbar_col_b[:], kbar_col[:]), inc=s_v, waits=[(s_g, s_g.n)])

        for p in range(2):
            for hf in range(2):
                c.do(VE, lambda e, p=p, hf=hf: e.tensor_single_scalar(
                    xt[0][:], kT[:, p * S + hf * 1024: p * S + (hf + 1) * 1024],
                    kbar_col[:, p:p + 1], ALU.subtract), inc=s_v)
                c.do(VE, lambda e, p=p, hf=hf: e.tensor_tensor(
                    xt[0][:], xt[0][:], maskb[:, ts(hf, 1024)], ALU.mult), inc=s_v,
                     waits=[(s_v, s_v.n)])
                c.do(VE, lambda e, p=p, hf=hf: e.tensor_single_scalar(
                    k0m[:, p * SK + hf * 1024: p * SK + (hf + 1) * 1024], xt[0][:],
                    kbar_col[:, p:p + 1], ALU.add), inc=s_v, waits=[(s_v, s_v.n)])

        c.barrier(sems["b1"])
        if DEBUG and (DBGSEL >> 1) & 1:
            c.do(SY, lambda e: e.dma_start(out=dbg["d_cnt"], in_=cnt_row[:]), inc=s_st, inck=16)
            c.do(SY, lambda e: e.dma_start(out=dbg["d_kbar"], in_=kbar_row[:]), inc=s_st, inck=16)

        for h in range(HL):
            c.do(VE, lambda e, h=h: e.memset(qTz[h][:], 0.0), inc=(s_v if h == 3 else None))
        ve_ck = []
        mm_ck = {}
        sc_rel = {}
        mst_ck = 0
        ci = 0
        for t in range(NT):
            for p in range(2):
                pst = PS_T if p == 0 else PS_T2
                c.do(TE, lambda e, t=t, p=p, pst=pst: e.transpose(
                    pst[:, 0:128],
                    qtok[:, t * QW + p * 128: t * QW + (p + 1) * 128], id_b[:]),
                    inc=(s_tp if p == 1 else None),
                    waits=([(s_s, s_s.n), (s_ev, s_ev.n)] if p == 0 else []))
            for h in range(HL):
                p, par = h // 2, h % 2
                pst = PS_T if p == 0 else PS_T2
                c.do(SC, lambda e, h=h, par=par, pst=pst: e.copy(
                    out=qTz[h][par * 64: par * 64 + 64, :],
                    in_=pst[par * 64: par * 64 + 64, 0:128]),
                    inc=(s_ev if h == 3 else None),
                    waits=([(s_tp, s_tp.n), (s_v, s_v.n if t == 0 else 0)] if h == 0 else [])
                    + ([(s_mm, mm_ck.get((t - 1, h), 0))]))
            for h in range(HL):
                p = h // 2
                for j in range(2):
                    pi = ci % 2
                    if ci == 0:
                        wrel = [(s_v, s_v.n)]
                    elif ci >= 2:
                        wrel = [ve_ck[ci - 2]]
                    else:
                        wrel = []
                    for kk in range(2):
                        c.do(TE, lambda e, h=h, p=p, j=j, kk=kk, pi=pi: e.matmul(
                            PS_A[:, pi * 1024 + kk * 512: pi * 1024 + (kk + 1) * 512],
                            qTz[h][:], k0m[:, p * SK + j * 1024 + kk * 512: p * SK + j * 1024 + (kk + 1) * 512],
                            start=True, stop=True),
                            inc=(s_mm if (j == 1 and kk == 1) else None),
                            waits=(([(s_ev, s_ev.n)] + wrel) if kk == 0 else []))
                    if j == 0:
                        c.do(TE, lambda e, h=h, p=p: e.matmul(
                            PS_B[:, 0:1], qTz[h][:], kbar_col_b[:, p:p + 1],
                            start=True, stop=True), inc=s_mm)
                        scb = (t * 4 + h) % 2
                        c.do(SC, lambda e, pi=pi, scb=scb: e.activation(
                            out=yt[scb][:], in_=PS_A[:, pi * 1024: (pi + 1) * 1024],
                            func=AF.Identity, scale=1.0),
                            inc=s_s, waits=[(s_mm, s_mm.n), (s_v, sc_rel.get(ci - 4, 0))])
                        ve_ck.append((s_s, s_s.n))
                        c.do(VE, lambda e, t=t, h=h: e.tensor_copy(
                            mmean[:, h * NT + t: h * NT + t + 1], PS_B[:, 0:1]),
                            waits=[(s_mm, s_mm.n)])
                        c.do(VE, lambda e, scb=scb: e.tensor_scalar(
                            etile[:, 0:1024], yt[scb][:], 0.0, None,
                            ALU.add, ALU.max, accum_out=mst[:, h: h + 1]),
                            inc=s_v, waits=[(s_s, s_s.n)])
                        sc_rel[ci] = s_v.n
                        mst_ck = s_v.n
                    else:
                        mm_ck[(t, h)] = s_mm.n
                        c.do(VE, lambda e: e.tensor_scalar_add(vfb[:], vfb[:], 0.0),
                             waits=[(s_v, mst_ck)])
                        c.do(VE, lambda e: e.tensor_scalar_add(vfb[:], vfb[:], 0.0))
                        c.do(VE, lambda e, pi=pi, t=t, h=h: e.tensor_scalar(
                            etile[:, 0:1024], PS_A[:, pi * 1024: (pi + 1) * 1024],
                            mst[:, h: h + 1], None,
                            ALU.max, ALU.max, accum_out=mmax[:, h * NT + t: h * NT + t + 1]),
                            inc=s_v, waits=[(s_mm, s_mm.n)])
                        ve_ck.append((s_v, s_v.n))
                    ci += 1
        vfence()
        c.do(VE, lambda e: e.tensor_tensor(mmat[:], mmax[:], mmean[:], ALU.subtract), inc=s_v,
             waits=[(s_v, s_v.n)])
        if DEBUG and (DBGSEL >> 2) & 1:
            c.do(GP, lambda e: e.dma_start(out=dbg["d_qtz"], in_=qTz[0][:]), inc=s_st, inck=16,
                 waits=[(s_mm, s_mm.n)])

        c.barrier(sems["b2"])
        if DEBUG and (DBGSEL >> 3) & 1:
            c.do(SY, lambda e: e.dma_start(out=dbg["d_M"], in_=mmat[:]), inc=s_st, inck=16)
            c.do(SY, lambda e: e.nop(), waits=[(s_st, s_st.n)])
            c.do(GP, lambda e: e.dma_start(out=dbg["d_kt"], in_=kT[:, 0:128]), inc=s_st, inck=16)
            c.do(GP, lambda e: e.dma_start(out=dbg["d_qt"], in_=qtok[:, 0:QW]), inc=s_st, inck=16)
            c.do(GP, lambda e: e.dma_start(out=dbg["d_k0m"], in_=k0m[:, 0:256]), inc=s_st, inck=16)
            c.do(SY, lambda e: e.dma_start(out=dbg["d_mmax"], in_=mmax[:]), inc=s_st, inck=16)
            c.do(SY, lambda e: e.dma_start(out=dbg["d_mmean"], in_=mmean[:]), inc=s_st, inck=16)
            c.do(GP, lambda e: e.dma_start(out=dbg["d_mask"], in_=maskb[:, 0:256]), inc=s_st, inck=16)

        for h in range(HL):
            c.do(VE, lambda e, h=h: e.max(out=mwork[:, h * 8: h * 8 + 8], in_=mmat[:, ts(h, NT)]),
                 inc=s_v, waits=[(s_v, s_v.n)])
        vfence()
        c.do(VE, lambda e: e.tensor_copy(etile[:, 0:32], mwork[:, 0:32]), inc=s_v,
             waits=[(s_v, s_v.n)])
        c.do(GP, lambda e: e.dma_start(out=cand_bounce.ap(), in_=mwork[:, 0:32]), inc=s_g, inck=16,
             waits=[(s_v, s_v.n)])
        for h in range(HL):
            c.do(GP, lambda e, h=h: e.dma_start(out=candf4[h:h + 1, :],
                                                in_=cand_bounce.ap()[:, h * 8:(h + 1) * 8]),
                 inc=s_g, inck=16, waits=([(s_g, s_g.n)] if h == 0 else []))
        if DEBUG and (DBGSEL >> 4) & 1:
            c.do(SY, lambda e: e.dma_start(out=dbg["d_cand"], in_=candf4[:]), inc=s_st, inck=16,
                 waits=[(s_g, s_g.n)])
            c.do(SY, lambda e: e.nop(), waits=[(s_st, s_st.n)])
        for r in range(6):
            c.do(VE, lambda e, r=r: e.max(out=topsq[:, r * 8: (r + 1) * 8], in_=candf4[:]),
                 inc=s_v, waits=[(s_v, s_v.n)] + ([(s_g, s_g.n)] if r == 0 else []))
            if r < 5:
                c.do(VE, lambda e, r=r: e.match_replace(out=candf4[:],
                                                        in_to_replace=topsq[:, r * 8: (r + 1) * 8],
                                                        in_values=candf4[:], imm_value=-3e38),
                     inc=s_v, waits=[(s_v, s_v.n)])
        if DEBUG and (DBGSEL >> 5) & 1:
            c.do(SY, lambda e: e.dma_start(out=dbg["d_tops"], in_=topsq[:]), inc=s_st, inck=16,
                 waits=[(s_v, s_v.n)])
            c.do(SY, lambda e: e.nop(), waits=[(s_st, s_st.n)])
        c.do(VE, lambda e: e.tensor_single_scalar(d4[:], id_f[0:HL, 0:HL], topsq[:, 44:45], ALU.mult),
             inc=s_v, waits=[(s_v, s_v.n)])
        c.do(TE, lambda e: e.matmul(PS_B[:, 0:HL], onesf[0:HL, :], d4[:], start=True, stop=True),
             inc=s_mm, waits=[(s_v, s_v.n)])
        c.do(VE, lambda e: e.tensor_copy(thr_c[:, 0:HL], PS_B[:, 0:HL]), inc=s_v,
             waits=[(s_mm, s_mm.n)])
        if DEBUG and (DBGSEL >> 6) & 1:
            c.do(SY, lambda e: e.dma_start(out=dbg["d_thr"], in_=thr_c[0:1, 0:HL]), inc=s_st, inck=16,
                 waits=[(s_v, s_v.n)])

        c.do(VE, lambda e: e.memset(payload[:], 0.0), inc=s_v)
        c.do(VE, lambda e: e.memset(qselT[:], 0.0), inc=s_v)
        for h in range(HL):
            par, p = h % 2, h // 2
            sl = slice(par * 64, par * 64 + 64)
            c.do(VE, lambda e, h=h: e.tensor_single_scalar(maskM[:], mmat[:, ts(h, NT)], thr_c[:, h:h + 1],
                                                           ALU.is_ge), inc=s_v, waits=[(s_v, s_v.n)])
            c.do(VE, lambda e: e.tensor_tensor_scan(rankM[:], maskM[:], zer64[:], 0.0, ALU.add, ALU.add),
                 inc=s_v, waits=[(s_v, s_v.n)])
            c.do(TE, lambda e: e.transpose(PS_B[0:1, 0:128], rankM[:, NT - 1: NT], id_f[:]),
                 inc=s_mm, waits=[(s_v, s_v.n)])
            c.do(VE, lambda e: e.tensor_copy(rtot[:], PS_B[0:1, 0:128]), inc=s_v, waits=[(s_mm, s_mm.n)])
            c.do(VE, lambda e: e.tensor_tensor_scan(rsc[:], rtot[:], zrow[:], 0.0, ALU.add, ALU.add),
                 inc=s_v)
            c.do(VE, lambda e: e.tensor_tensor(rsc[:], rsc[:], rtot[:], ALU.subtract), inc=s_v,
                 waits=[(s_v, s_v.n)])
            c.do(TE, lambda e: e.transpose(PS_B[:, 0:1], rsc[:], one1f[:]), inc=s_mm, waits=[(s_v, s_v.n)])
            c.do(VE, lambda e: e.tensor_copy(roff_c[:], PS_B[:, 0:1]), inc=s_v, waits=[(s_mm, s_mm.n)])
            c.do(VE, lambda e: e.tensor_single_scalar(rankM[:], rankM[:], roff_c[:, 0:1], ALU.add),
                 inc=s_v, waits=[(s_v, s_v.n)])
            if DEBUG and h == 0 and (DBGSEL >> 21) & 1:
                c.do(SY, lambda e: e.dma_start(out=dbg["d_rank"], in_=rankM[:]), inc=s_st, inck=16,
                     waits=[(s_v, s_v.n)])
                c.do(SY, lambda e: e.dma_start(out=dbg["d_maskM"], in_=maskM[:]), inc=s_st, inck=16)
                c.do(SY, lambda e: e.nop(), waits=[(s_st, s_st.n)])
            vbase = s_v.n
            for t in range(NT):
                eb = t % 2
                c.do(VE, lambda e, t=t, eb=eb: e.tensor_scalar(et_s[eb][:], jconst_s[:], rankM[:, t:t + 1],
                                                               maskM[:, t:t + 1], ALU.is_equal, ALU.mult),
                     inc=s_v, waits=([(s_mm, s_mm.n)] if t >= 2 else [(s_v, s_v.n)]))
                c.do(TE, lambda e, t=t, h=h, eb=eb: e.matmul(
                    PS_C[0:48, 0:64], et_s[eb][:], qtok[:, t * QW + h * 64: t * QW + (h + 1) * 64],
                    start=(t == 0), stop=(t == NT - 1)),
                    waits=([(s_v, vbase + t + 1)] + ([(s_s, s_s.n)] if t == 0 else [])))
                c.do(TE, lambda e, t=t, eb=eb: e.matmul(
                    PS_B[0:48, 0:3], et_s[eb][:], qtok[:, t * QW + 256: t * QW + 259],
                    start=(t == 0), stop=(t == NT - 1)),
                    inc=s_mm)
            c.do(VE, lambda e: e.tensor_copy(qsel_r[:], PS_C[0:48, 0:64]), waits=[(s_mm, s_mm.n)])
            c.do(VE, lambda e: e.tensor_copy(extr[:], PS_B[0:48, 0:3]), inc=s_v)
            if DEBUG and h == 0 and (DBGSEL >> 22) & 1:
                c.do(GP, lambda e: e.dma_start(out=dbg["d_et"], in_=et_s[1][:]), inc=s_st, inck=16,
                     waits=[(s_v, s_v.n)])
                c.do(SY, lambda e: e.dma_start(out=dbg["d_extr"], in_=extr[:]), inc=s_st, inck=16,
                     waits=[(s_v, s_v.n)])
                c.do(SY, lambda e: e.nop(), waits=[(s_st, s_st.n)])
            c.do(TE, lambda e: e.transpose(PS_C[0:64, 0:48], qsel_r[:], id_f[0:48, 0:48]),
                 inc=s_mm, waits=[(s_v, s_v.n)])
            if par == 0:
                c.do(SC, lambda e: e.copy(out=qselT[0:64, :], in_=PS_C[0:64, 0:48]), inc=s_s,
                     waits=[(s_mm, s_mm.n)])
            else:
                c.do(SC, lambda e: e.copy(out=qsel_st[:], in_=PS_C[0:64, 0:48]),
                     waits=[(s_mm, s_mm.n)])
                c.do(SC, lambda e: e.copy(out=barbuf[0:1, 6:7], in_=barbuf[0:1, 7:8]))
                c.do(SC, lambda e: e.copy(out=qsel_st[:], in_=qsel_st[:]), inc=s_s)
                c.do(GP, lambda e: e.dma_start(out=qselT[64:128, :], in_=qsel_st[:]),
                     inc=s_g, inck=16, waits=[(s_s, s_s.n)])
                c.do(SC, lambda e: e.copy(out=zer64[0:1, 0:1], in_=zer64[0:1, 0:1]), inc=s_s,
                     waits=[(s_g, s_g.n)])
            c.do(VE, lambda e: e.scalar_tensor_tensor(pcol[:], extr[:, 2:3], 128.0, extr[:, 1:2],
                                                      ALU.mult, ALU.add), inc=s_v,
                 waits=[(s_v, s_v.n)])
            c.do(VE, lambda e, h=h: e.scalar_tensor_tensor(qid_c[:, h:h + 1], extr[:, 0:1], -1e6, pcol[:],
                                                           ALU.mult, ALU.add), inc=s_v,
                 waits=[(s_v, s_v.n)])
            c.do(VE, lambda e, h=h: e.tensor_scalar(qid_c[:, h:h + 1], qid_c[:, h:h + 1], 1e6, None, ALU.add),
                 inc=s_v, waits=[(s_v, s_v.n)])
            c.do(TE, lambda e: e.transpose(PS_B[0:1, 0:48], extr[:, 0:1], id_f[0:48, 0:48]),
                 inc=s_mm, waits=[(s_v, s_v.n)])
            c.do(VE, lambda e: e.tensor_copy(padrow[:], PS_B[0:1, 0:48]), inc=s_v, waits=[(s_mm, s_mm.n)])
            vb2 = s_v.n
            sb2 = s_s.n
            for u in range(NT):
                psl = PS_A[:, 0:48] if u % 2 == 0 else PS_A[:, 512:560]
                c.do(TE, lambda e, u=u, p=p, sl=sl, psl=psl: e.matmul(
                    psl, kT[sl, p * S + u * 128: p * S + (u + 1) * 128], qselT[sl, :],
                    start=True, stop=True), inc=s_mm,
                    waits=([(s_s, s_s.n), (s_v, s_v.n)] if u == 0 else [(s_v, vb2 + u - 1)]))
                c.do(VE, lambda e, u=u, psl=psl: e.tensor_scalar(
                    etile[:, 0:48], psl, 0.0, None, ALU.add, ALU.max,
                    accum_out=m2acc[:, u:u + 1]), inc=s_v, waits=[(s_mm, s_mm.n)])
            vfence()
            c.do(VE, lambda e: e.reduce_max(m2b[:, 0:1], m2acc[:], axis=AX), inc=s_v,
                 waits=[(s_v, s_v.n)])
            c.do(TE, lambda e: e.transpose(PS_B[0:1, 0:128], m2b[:], id_f[:]), inc=s_mm,
                 waits=[(s_v, s_v.n)])
            c.do(VE, lambda e: e.reduce_max(m2fin[:], PS_B[0:1, 0:128], axis=AX), inc=s_v,
                 waits=[(s_mm, s_mm.n)])
            vfence()
            c.do(VE, lambda e: e.tensor_scalar(m2fin[:], m2fin[:], -0.125, None, ALU.mult), inc=s_v,
                 waits=[(s_v, s_v.n)])
            vfence()
            c.do(VE, lambda e: e.tensor_copy(m2fin[:], m2fin[:]), inc=s_v,
                 waits=[(s_v, s_v.n)])
            c.do(TE, lambda e: e.matmul(PS_B[:, 0:1], onesf[0:1, :], m2fin[:], start=True, stop=True),
                 inc=s_mm, waits=[(s_v, s_v.n)])
            c.do(VE, lambda e: e.tensor_copy(m2b[:], PS_B[:, 0:1]), inc=s_v, waits=[(s_mm, s_mm.n)])
            c.do(VE, lambda e: e.memset(expT[0][:], 0.0))
            c.do(VE, lambda e: e.memset(expT[1][:], 0.0))
            c.do(VE, lambda e: e.tensor_copy(expT[0][:, 48:49], onesb[:, 0:1]))
            c.do(VE, lambda e: e.tensor_copy(expT[1][:, 48:49], onesb[:, 0:1]), inc=s_v)
            sb3 = s_s.n
            mmb3 = s_mm.n
            for u in range(NT):
                pu = u % 2
                psl = PS_A[:, 0:48] if pu == 0 else PS_A[:, 512:560]
                c.do(TE, lambda e, u=u, p=p, sl=sl, psl=psl: e.matmul(
                    psl, kT[sl, p * S + u * 128: p * S + (u + 1) * 128], qselT[sl, :],
                    start=True, stop=True), inc=s_mm,
                    waits=([(s_v, s_v.n)] if u == 0 else [(s_s, sb3 + u - 1)]))
                c.do(SC, lambda e, u=u, pu=pu, psl=psl: e.activation(
                    out=expT[pu][:, 0:48], in_=psl, func=AF.Exp, bias=m2b[:, 0:1], scale=0.125),
                    inc=s_s, waits=[(s_mm, mmb3 + 2 * u + 1), (s_mm, max(mmb3, mmb3 + 2 * (u - 2) + 2))])
                c.do(TE, lambda e, u=u, h=h, pu=pu: e.matmul(
                    PS_C[0:65, 0:49], vtok[:, (h * NT + u) * 65: (h * NT + u) * 65 + 65], expT[pu][:],
                    start=(u == 0), stop=(u == NT - 1)), inc=s_mm,
                    waits=[(s_s, sb3 + u + 1)])
            c.do(VE, lambda e: e.tensor_copy(ctxTs[0:65, :], PS_C[0:65, 0:49]), inc=s_v,
                 waits=[(s_mm, s_mm.n)])
            c.do(VE, lambda e: e.reciprocal(recipr[64:65, :], ctxTs[64:65, 0:48]), inc=s_v)
            c.do(VE, lambda e: e.tensor_scalar_add(vfb[:], vfb[:], 0.0), waits=[(s_v, s_v.n)])
            c.do(VE, lambda e: e.tensor_copy(recipr[64:65, :], recipr[64:65, :]), inc=s_v)
            c.do(GP, lambda e: e.dma_start(out=recipr[0:1, :], in_=recipr[64:65, :]), inc=s_g, inck=16,
                 waits=[(s_v, s_v.n)])
            c.do(VE, lambda e: e.tensor_tensor(recipr[0:1, :], recipr[0:1, :], padrow[:], ALU.mult),
                 inc=s_v, waits=[(s_g, s_g.n)])
            c.do(TE, lambda e: e.matmul(PS_B[0:64, 0:48], onesf[0:1, 0:64], recipr[0:1, :],
                                        start=True, stop=True), waits=[(s_v, s_v.n)])
            c.do(TE, lambda e: e.matmul(PS_B[0:64, 64:112], onesf[0:1, 0:64], padrow[:],
                                        start=True, stop=True), inc=s_mm)
            c.do(VE, lambda e: e.tensor_copy(recb[:], PS_B[0:64, 0:48]), waits=[(s_mm, s_mm.n)])
            c.do(VE, lambda e: e.tensor_copy(padb[:], PS_B[0:64, 64:112]), inc=s_v)
            c.do(VE, lambda e: e.tensor_scalar(vm_c[:], ctxTs[0:64, 48:49], 1.0 / S, None, ALU.mult),
                 inc=s_v)
            c.do(VE, lambda e: e.tensor_tensor(t48a[:], ctxTs[0:64, 0:48], recb[:], ALU.mult))
            c.do(VE, lambda e: e.tensor_single_scalar(t48b[:], padb[:], vm_c[:, 0:1], ALU.mult),
                 inc=s_v, waits=[(s_v, s_v.n)])
            c.do(VE, lambda e: e.tensor_tensor(t48a[:], t48a[:], t48b[:], ALU.subtract), inc=s_v,
                 waits=[(s_v, s_v.n)])
            c.do(VE, lambda e, h=h: e.tensor_copy(payload[0:64, h * 50: h * 50 + 48], t48a[:]))
            c.do(VE, lambda e, h=h: e.tensor_copy(payload[0:64, h * 50 + 48: h * 50 + 49], vm_c[:]),
                 inc=s_v)

        c.barrier(sems["b3"])
        if DEBUG and (DBGSEL >> 7) & 1:
            c.do(SY, lambda e: e.dma_start(out=dbg["d_qid"], in_=qid_c[:]), inc=s_st, inck=16)
            c.do(VE, lambda e: e.tensor_copy(mwork[:, 0:200], payload[:]), inc=s_v)
            c.do(SY, lambda e: e.dma_start(out=dbg["d_pay"], in_=mwork[:, 0:200]), inc=s_st, inck=16,
                 waits=[(s_v, s_v.n)])

        c.do(GP, lambda e: e.dma_start(out=cc1_in.ap(), in_=payload[:]), inc=s_g, inck=16)
        c.do(GP, lambda e: e.dma_start(out=cc2_in.ap(), in_=qid_c[:]), inc=s_g, inck=16)
        c.do(GP, lambda e: e.collective_compute(
            "AllGather", ALU.bypass, replica_groups=[[0, 1, 2, 3], [4, 5, 6, 7]],
            ins=[cc1_in.ap().opt()], outs=[cc1_out.ap().opt()]), inc=s_cc,
            waits=[(s_g, s_g.n)])
        c.do(GP, lambda e: e.collective_compute(
            "AllGather", ALU.bypass, replica_groups=[[0, 1, 2, 3], [4, 5, 6, 7]],
            ins=[cc2_in.ap().opt()], outs=[cc2_out.ap().opt()]), inc=s_cc)
        c.do(GP, lambda e: e.dma_start(out=pay_all[:].rearrange("p (r w) -> p r w", r=4),
                                       in_=cc1_out.ap().rearrange("(r p) w -> p r w", p=128)),
             inc=s_g, inck=16, waits=[(s_cc, s_cc.n)])

        c.barrier(sems["b4"])

        if DEBUG and (DBGSEL >> 8) & 1:
            c.do(GP, lambda e: e.dma_start(out=dbg["d_payall"], in_=pay_all[:]), inc=s_st, inck=16)
            c.do(GP, lambda e: e.dma_start(out=dbg["d_payall"], in_=pay_all[:]) if False else e.dma_start(
                out=cc1_in.ap(), in_=payload[:]), inc=s_st, inck=16) if False else None
        c.do(SY, lambda e: e.dma_start(out=bo_s[:], in_=bo_r), inc=s_ld, inck=16)
        c.do(VE, lambda e: e.memset(qidcol8[:], 5e6), inc=s_v)
        for a in range(2):
            for peer in range(4):
                c.do(GP, lambda e, a=a, peer=peer: e.dma_start(
                    out=qidcol8[a * 64: a * 64 + 48, peer * 2: peer * 2 + 2],
                    in_=cc2_out.ap()[peer * 48: peer * 48 + 48, a::2],
                    allow_slow_non_contiguous=True),
                    inc=s_g, inck=16, waits=([(s_v, s_v.n)] if a == 0 and peer == 0 else []))
        c.do(TE, lambda e: e.matmul(PS_B[:, 0:1], onesf[0:1, :], tok0_s[:], start=True, stop=True),
             inc=s_mm, waits=[(s_ld, s_ld.n)])
        c.do(VE, lambda e: e.tensor_copy(tok0b[:], PS_B[:, 0:1]), inc=s_v, waits=[(s_mm, s_mm.n)])
        c.do(VE, lambda e: e.tensor_single_scalar(qsh8[:], qidcol8[:], tok0b[:, 0:1], ALU.subtract),
             inc=s_v, waits=[(s_g, s_g.n), (s_v, s_v.n)])

        c.do(VE, lambda e: e.memset(corr[:], 0.0), inc=s_v)
        woR = wo.rearrange("(g p) j -> p g j", p=64)
        for half in range(2):
            c.do(GP, lambda e, half=half: e.dma_start(
                out=wo_b[:], in_=woR[:, half * 8: (half + 1) * 8, :]),
                inc=s_g, inck=16,
                waits=([(s_mm, s_mm.n), (s_ev, s_ev.n)] if half == 1 else []))
            for gl in range(8):
                g = half * 8 + gl
                peer, hh = g // 4, g % 4
                pslice = pay_all[0:64, peer * 200 + hh * 50: peer * 200 + hh * 50 + 48]
                vslice = pay_all[0:64, peer * 200 + hh * 50 + 48: peer * 200 + hh * 50 + 49]
                for kk in range(2):
                    c.do(TE, lambda e, gl=gl, g=g, kk=kk, vslice=vslice: e.matmul(
                        PS_A[0:1, 0:512] if kk == 0 else PS_A[0:1, 512:1024],
                        vslice, wo_b[:, gl * D + kk * 512: gl * D + (kk + 1) * 512],
                        start=(g == 0), stop=(g == 15)),
                        waits=([(s_g, s_g.n)] if (gl == 0 and kk == 0) else []))
                for kk in range(2):
                    c.do(TE, lambda e, gl=gl, g=g, kk=kk, pslice=pslice: e.matmul(
                        PS_A[0:48, 1024 + kk * 512: 1024 + (kk + 1) * 512],
                        pslice, wo_b[:, gl * D + kk * 512: gl * D + (kk + 1) * 512],
                        start=True, stop=True),
                        inc=(s_mm if kk == 1 else None),
                        waits=([(s_ev, s_ev.n), (s_v, s_v.n),
                                (s_g, s_g.n)] if kk == 0 else []))
                if g % 2 == 0:
                    c.do(SC, lambda e, g=g: e.copy(
                        out=corr[0:48, (g // 2) * D: (g // 2 + 1) * D], in_=PS_A[0:48, 1024:2048]),
                        inc=s_ev, waits=[(s_mm, s_mm.n)])
                else:
                    c.do(SC, lambda e, g=g: e.copy(
                        out=corr_st[:, (g // 2) * D: (g // 2 + 1) * D], in_=PS_A[0:48, 1024:2048]),
                        waits=[(s_mm, s_mm.n)])
                    c.do(SC, lambda e: e.copy(out=barbuf[0:1, 6:7], in_=barbuf[0:1, 7:8]))
                    c.do(SC, lambda e, g=g: e.copy(
                        out=corr_st[:, (g // 2) * D: (g // 2 + 1) * D],
                        in_=corr_st[:, (g // 2) * D: (g // 2 + 1) * D]), inc=s_ev)
                    c.do(GP, lambda e, g=g: e.dma_start(
                        out=corr[64:112, (g // 2) * D: (g // 2 + 1) * D],
                        in_=corr_st[:, (g // 2) * D: (g // 2 + 1) * D]),
                        inc=s_g, inck=16, waits=[(s_ev, s_ev.n)])
        if DEBUG and (DBGSEL >> 9) & 1:
            c.do(GP, lambda e: e.dma_start(out=dbg["d_wob"], in_=wo_b[:, 0:1024]), inc=s_st, inck=16)
            c.do(GP, lambda e: e.dma_start(out=dbg["d_corr"], in_=corr[:, 0:1024]), inc=s_st, inck=16)
            c.do(GP, lambda e: e.dma_start(out=cc1_in.ap()[0:1, 0:1], in_=onesb[0:1, 0:1]),
                 inc=s_st, inck=16)
            c.do(SY, lambda e: e.nop(), waits=[(s_st, s_st.n)])
        c.do(VE, lambda e: e.tensor_copy(rb_row[:, 0:512], PS_A[0:1, 0:512]),
             waits=[(s_mm, s_mm.n), (s_ev, s_ev.n)])
        c.do(VE, lambda e: e.tensor_copy(rb_row[:, 512:1024], PS_A[0:1, 512:1024]))
        c.do(VE, lambda e: e.tensor_tensor(rb_row[:], rb_row[:], bo_s[:], ALU.add),
             waits=[(s_ld, s_ld.n)])
        c.do(VE, lambda e: e.tensor_copy(rb_row_b[:], rb_row[:]), inc=s_v)
        if DEBUG and (DBGSEL >> 10) & 1:
            c.do(SY, lambda e: e.dma_start(out=dbg["d_rb"], in_=rb_row[:]), inc=s_st, inck=16,
                 waits=[(s_v, s_v.n)])


        ev_b = s_ev.n
        if DEBUG and (DBGSEL >> 11) & 1:
            c.do(GP, lambda e: e.dma_start(out=dbg["d_qsh"], in_=qsh8[:]), inc=s_st, inck=16,
                 waits=[(s_v, s_v.n)])
            c.do(GP, lambda e: e.dma_start(out=dbg["d_corr6"], in_=corr[:, 6 * D:7 * D]),
                 inc=s_st, inck=16)
            c.do(GP, lambda e: e.dma_start(out=dbg["d_corr6b"], in_=corr[:, 6 * D:7 * D]),
                 inc=s_st, inck=16, waits=[(s_g, s_g.n), (s_ev, s_ev.n)])
            c.do(SY, lambda e: e.nop(), waits=[(s_st, s_st.n)])
        for t in range(NTB):
            for cc in range(8):
                eb = cc % 2
                c.do(VE, lambda e, t=t, cc=cc, eb=eb: e.tensor_scalar(
                    esc[eb][:], iota128r_s[:], float(128 * t), qsh8[:, cc:cc + 1], ALU.add, ALU.is_equal),
                    inc=s_v, waits=([(s_mm, s_mm.n)] if cc >= 2 or t > 0 else []))
                if DEBUG and t == 0 and cc == 6 and (DBGSEL >> 23) & 1:
                    c.do(GP, lambda e, eb=eb: e.dma_start(out=dbg["d_esc"], in_=esc[eb][:]),
                         inc=s_st, inck=16, waits=[(s_v, s_v.n)])
                    c.do(SY, lambda e: e.nop(), waits=[(s_st, s_st.n)])
                for kk in range(2):
                    c.do(TE, lambda e, t=t, cc=cc, eb=eb, kk=kk: e.matmul(
                        PS_A[:, kk * 512: (kk + 1) * 512], esc[eb][:],
                        corr[:, cc * D + kk * 512: cc * D + (kk + 1) * 512],
                        start=(cc == 0), stop=False),
                        inc=(s_mm if kk == 1 else None),
                        waits=([(s_v, s_v.n), (s_ev, s_ev.n), (s_g, s_g.n)] if kk == 0 else []))
            for kk in range(2):
                c.do(TE, lambda e, kk=kk: e.matmul(
                    PS_A[:, kk * 512: (kk + 1) * 512], onesb[0:1, :], rb_row_b[:, kk * 512: (kk + 1) * 512],
                    start=False, stop=True), inc=(s_mm if kk == 1 else None))
            c.do(SY, lambda e, t=t: e.dma_start(out=xtb[t % 2][:], in_=xo[ts(t, 128), :]),
                 inc=s_x, inck=16, waits=([(s_v, s_v.n)] if t >= 2 else []))
            c.do(VE, lambda e, t=t: e.tensor_tensor(x2[:, ts(t, D)], xtb[t % 2][:], PS_A[:, 0:1024], ALU.add),
                 inc=s_v, waits=[(s_x, s_x.n), (s_mm, s_mm.n)])
        if DEBUG and (DBGSEL >> 12) & 1:
            c.do(SY, lambda e: e.dma_start(out=dbg["d_x2"], in_=x2[:, 0:D]), inc=s_st, inck=16,
                 waits=[(s_v, s_v.n)])

        for t in range(NTB):
            c.do(VE, lambda e, t=t: e.reduce_sum(x2sum[:, t:t + 1], x2[:, ts(t, D)], axis=AX), inc=s_v)
            c.do(SC, lambda e, t=t: e.activation(out=zb[:], in_=x2[:, ts(t, D)], func=AF.Square,
                                                 accum_out=x2sq[:, t:t + 1]), inc=s_s,
                 waits=[(s_v, s_v.n)] if t == 0 else [])
        c.do(VE, lambda e: e.tensor_scalar(rstd2[:], x2sum[:], 1.0 / D, None, ALU.mult),
             inc=s_v, waits=[(s_v, s_v.n), (s_s, s_s.n)])
        c.do(VE, lambda e: e.tensor_tensor(mrs2[:], rstd2[:], rstd2[:], ALU.mult), inc=s_v,
             waits=[(s_v, s_v.n)])
        c.do(VE, lambda e: e.scalar_tensor_tensor(mrs2[:], x2sq[:], 1.0 / D, mrs2[:], ALU.mult, ALU.subtract),
             inc=s_v, waits=[(s_v, s_v.n)])
        c.do(SC, lambda e: e.activation(out=mrs2[:], in_=mrs2[:], func=AF.Sqrt, bias=1e-5, scale=1.0),
             inc=s_s, waits=[(s_v, s_v.n)])
        c.do(VE, lambda e: e.tensor_scalar(x2sum[:], x2sum[:], 1.0 / D, None, ALU.mult),
             inc=s_v, waits=[(s_s, s_s.n)])
        c.do(VE, lambda e: e.reciprocal(rstd2[:], mrs2[:]), inc=s_v, waits=[(s_v, s_v.n)])
        c.do(VE, lambda e: e.tensor_tensor(mrs2[:], x2sum[:], rstd2[:], ALU.mult), inc=s_v,
             waits=[(s_v, s_v.n)])
        c.do(VE, lambda e: e.tensor_scalar(mrs2[:], mrs2[:], -1.0, None, ALU.mult), inc=s_v,
             waits=[(s_v, s_v.n)])
        for dst, src in [(gw1_s, gw1), (gw2_s, gw2)]:
            c.do(GP, lambda e, dst=dst, src=src: e.dma_start(
                out=dst[:], in_=src.rearrange("(c p) j -> p c j", p=128)), inc=s_g, inck=16,
                waits=[(s_mm, s_mm.n), (s_ev, s_ev.n), (s_v, s_v.n)])
        c.do(GP, lambda e: e.dma_start(out=gw3_cs[:], in_=gw3_c), inc=s_g, inck=16)
        c.do(GP, lambda e: e.dma_start(out=cw4_cs[:], in_=cw4_c), inc=s_g, inck=16)
        for dst, src in [(cb1_cs, cb1_c), (cb2_cs, cb2_c), (cb3_cs, cb3_c), (cb4_ss, cb4_s),
                         (gb1_rs, gb1_r), (gb2_rs, gb2_r), (gb3_ss, gb3_s)]:
            c.do(SY, lambda e, dst=dst, src=src: e.dma_start(out=dst[:], in_=src), inc=s_ld, inck=16)
        nload2 = s_ld.n
        for t in range(NTB):
            c.do(SC, lambda e, t=t: e.activation(out=zb[:], in_=x2[:, ts(t, D)], func=AF.Identity,
                                                 bias=mrs2[:, t:t + 1], scale=rstd2[:, t:t + 1]),
                 inc=s_s, waits=[(s_mm, s_mm.n), (s_v, s_v.n)] if t == 0 else [(s_mm, s_mm.n)])
            c.do(VE, lambda e, t=t: e.tensor_scalar(state[:, ts(t, D)], x2[:, ts(t, D)],
                                                    rstd2[:, t:t + 1], mrs2[:, t:t + 1],
                                                    ALU.mult, ALU.add), inc=s_v,
                 waits=([(s_v, s_v.n)] if t == 0 else []))
            for half, pq in ((0, PS_A[0:1, 0:512]), (1, PS_A[0:1, 512:1024])):
                c.do(TE, lambda e, t=t, half=half, pq=pq: e.matmul(
                    pq, onesb[:, 0:1], zb[:, ts(half, 512)],
                    start=(t == 0), stop=(t == NTB - 1)),
                    inc=(s_mm if half == 1 else None),
                    waits=([(s_s, s_s.n)] if half == 0 else []))
        c.do(VE, lambda e: e.tensor_copy(zcs[:, 0:512], PS_A[0:1, 0:512]), waits=[(s_mm, s_mm.n)])
        c.do(VE, lambda e: e.tensor_copy(zcs[:, 512:1024], PS_A[0:1, 512:1024]), inc=s_v)
        if DEBUG and (DBGSEL >> 13) & 1:
            c.do(SY, lambda e: e.dma_start(out=dbg["d_zcs"], in_=zcs[:]), inc=s_st, inck=16,
                 waits=[(s_v, s_v.n)])

        c.do(GP, lambda e: e.dma_start(out=cc3_in.ap(), in_=zcs[:]), inc=s_g, inck=16,
             waits=[(s_v, s_v.n)])
        c.do(GP, lambda e: e.collective_compute(
            "AllGather", ALU.bypass, replica_groups=[[0, 1, 2, 3], [4, 5, 6, 7]],
            ins=[cc3_in.ap().opt()], outs=[cc3_out.ap().opt()]), inc=s_cc,
            waits=[(s_g, s_g.n)])
        c.do(GP, lambda e: e.dma_start(out=gcrow[:], in_=cc3_out.ap()), inc=s_g, inck=16,
             waits=[(s_cc, s_cc.n)])
        for kk in range(2):
            c.do(TE, lambda e, kk=kk: e.matmul(PS_A[0:1, 0:512] if kk == 0 else PS_A[0:1, 512:1024],
                                               onesf[0:4, 0:1], gcrow[:, kk * 512: (kk + 1) * 512],
                                               start=True, stop=True),
                 inc=(s_mm if kk == 1 else None), waits=([(s_g, s_g.n)] if kk == 0 else []))
        c.do(SC, lambda e: e.activation(out=gcq[:, 0:512], in_=PS_A[0:1, 0:512], func=AF.Identity,
                                        scale=constv[1.0 / S][0:1, 0:1]), waits=[(s_mm, s_mm.n)])
        c.do(SC, lambda e: e.activation(out=gcq[:, 512:1024], in_=PS_A[0:1, 512:1024], func=AF.Identity,
                                        scale=constv[1.0 / S][0:1, 0:1]))
        c.do(SC, lambda e: e.copy(out=barbuf[0:1, 6:7], in_=barbuf[0:1, 7:8]))
        c.do(SC, lambda e: e.copy(out=gcq[:], in_=gcq[:]), inc=s_s)
        if DEBUG and (DBGSEL >> 14) & 1:
            c.do(SY, lambda e: e.dma_start(out=dbg["d_gcq"], in_=gcq[:]), inc=s_st, inck=16,
                 waits=[(s_s, s_s.n)])
        row2col(gc_col[:], gcq[:], D, waits=[(s_s, s_s.n)])
        for cc in range(8):
            c.do(TE, lambda e, cc=cc: e.matmul(PS_A[0:1, 0:512], gc_col[:, cc:cc + 1], gw1_s[:, ts(cc, 512)],
                                               start=(cc == 0), stop=(cc == 7)),
                 inc=(s_mm if cc == 7 else None), waits=([(s_g, s_g.n)] if cc == 0 else []))
        c.do(VE, lambda e: e.tensor_tensor(grow1[:], PS_A[0:1, 0:512], gb1_rs[:], ALU.add), inc=s_v,
             waits=[(s_mm, s_mm.n), (s_ld, nload2)])
        c.do(SC, lambda e: e.activation(out=grow1[:], in_=grow1[:], func=AF.Gelu),
             waits=[(s_v, s_v.n)])
        c.do(SC, lambda e: e.copy(out=barbuf[0:1, 6:7], in_=barbuf[0:1, 7:8]))
        c.do(SC, lambda e: e.copy(out=grow1[:], in_=grow1[:]), inc=s_s)
        if DEBUG and (DBGSEL >> 15) & 1:
            c.do(SY, lambda e: e.dma_start(out=dbg["d_grow1"], in_=grow1[:]), inc=s_st, inck=16,
                 waits=[(s_s, s_s.n)])
        row2col(gcol1[:], grow1[:], 512, waits=[(s_s, s_s.n)])
        for cc in range(4):
            c.do(TE, lambda e, cc=cc: e.matmul(PS_A[0:1, 0:256], gcol1[:, cc:cc + 1], gw2_s[:, ts(cc, 256)],
                                               start=(cc == 0), stop=(cc == 3)),
                 inc=(s_mm if cc == 3 else None), waits=([(s_g, s_g.n)] if cc == 0 else []))
        c.do(VE, lambda e: e.tensor_tensor(grow2[:], PS_A[0:1, 0:256], gb2_rs[:], ALU.add), inc=s_v,
             waits=[(s_mm, s_mm.n)])
        c.do(SC, lambda e: e.activation(out=grow2[:], in_=grow2[:], func=AF.Gelu),
             waits=[(s_v, s_v.n)])
        c.do(SC, lambda e: e.copy(out=barbuf[0:1, 6:7], in_=barbuf[0:1, 7:8]))
        c.do(SC, lambda e: e.copy(out=grow2[:], in_=grow2[:]), inc=s_s)
        if DEBUG and (DBGSEL >> 16) & 1:
            c.do(SY, lambda e: e.dma_start(out=dbg["d_grow2"], in_=grow2[:]), inc=s_st, inck=16,
                 waits=[(s_s, s_s.n)])
        row2col(gcol2[:], grow2[:], 256, waits=[(s_s, s_s.n)])
        for cc in range(2):
            c.do(TE, lambda e, cc=cc: e.matmul(PS_A[0:1, 0:1], gcol2[:, cc:cc + 1], gw3_cs[:, cc:cc + 1],
                                               start=(cc == 0), stop=(cc == 1)),
                 inc=(s_mm if cc == 1 else None), waits=([(s_g, s_g.n)] if cc == 0 else []))
        c.do(VE, lambda e: e.tensor_tensor(gf_s[:], PS_A[0:1, 0:1], gb3_ss[:], ALU.add), inc=s_v,
             waits=[(s_mm, s_mm.n)])
        c.do(SC, lambda e: e.activation(out=gf_s[:], in_=gf_s[:], func=AF.Sigmoid), inc=s_s,
             waits=[(s_v, s_v.n)])
        c.do(VE, lambda e: e.tensor_tensor(tmp1s[:], arng_s[:], gf_s[:], ALU.mult), inc=s_v,
             waits=[(s_s, s_s.n)])
        c.do(VE, lambda e: e.scalar_tensor_tensor(cA[:], tmp1s[:], 1.4, arng_s[:], ALU.mult, ALU.subtract),
             inc=s_v, waits=[(s_v, s_v.n)])
        c.do(VE, lambda e: e.tensor_scalar(cA[:], cA[:], 0.1, None, ALU.add), inc=s_v,
             waits=[(s_v, s_v.n)])
        c.do(VE, lambda e: e.tensor_scalar(cB[:], arng_s[:], 0.6, None, ALU.mult), inc=s_v)
        c.do(TE, lambda e: e.matmul(PS_B[:, 0:1], onesf[0:1, :], cA[:], start=True, stop=True),
             waits=[(s_v, s_v.n)])
        c.do(TE, lambda e: e.matmul(PS_B[:, 1:2], onesf[0:1, :], cB[:], start=True, stop=True),
             inc=s_mm)
        c.do(VE, lambda e: e.tensor_copy(cAb[:], PS_B[:, 0:1]), waits=[(s_mm, s_mm.n)])
        c.do(VE, lambda e: e.tensor_copy(cBb[:], PS_B[:, 1:2]), inc=s_v)
        c.barrier(sems["b6"])
        for dst, src_ in [(cw1_s, cw1), (cw2_s, cw2), (cw3_s, cw3)]:
            c.do(GP, lambda e, dst=dst, src_=src_: e.dma_start(
                out=dst[:], in_=src_.rearrange("(c p) j -> p c j", p=128)), inc=s_g, inck=16)
        if DEBUG and (DBGSEL >> 17) & 1:
            c.do(SY, lambda e: e.dma_start(out=dbg["d_gf"], in_=gf_s[:]), inc=s_st, inck=16)

        tf_g = {}
        for cyc in range(3):
            for q in range(4):
                for u in range(4):
                    tt = q * 4 + u
                    pst = PS_T if u % 2 == 0 else PS_T2
                    c.do(VE, lambda e, tt=tt: e.tensor_copy(sbuf_b[:], state[:, ts(tt, D)]),
                         inc=s_v, waits=[(s_tp, s_tp.n)])
                    for cc in range(8):
                        c.do(TE, lambda e, cc=cc, pst=pst: e.transpose(
                            pst[:, ts(cc, 128)], sbuf_b[:, ts(cc, 128)], id_b[:]),
                            inc=(s_tp if cc == 7 else None),
                            waits=([(s_v, s_v.n), (s_yt, max(0, s_yt.n - 1))] if cc == 0 else []))
                    for cc in range(8):
                        c.do(VE, lambda e, u=u, cc=cc, pst=pst: e.tensor_copy(
                            sT[:, cc * 512 + u * 128: cc * 512 + (u + 1) * 128],
                            pst[:, ts(cc, 128)]),
                            inc=(s_yt if cc == 7 else None),
                            waits=([(s_tp, s_tp.n)] if cc == 0 else []))
                for o in range(8):
                    pq = PS_B if o % 2 == 0 else PS_C
                    for cc in range(8):
                        c.do(TE, lambda e, o=o, cc=cc, pq=pq: e.matmul(
                            pq[:, 0:512], cw1_s[:, cc * D + o * 128: cc * D + (o + 1) * 128],
                            sT[:, ts(cc, 512)], start=(cc == 0), stop=(cc == 7)),
                            inc=(s_mm if cc == 7 else None),
                            waits=([(s_yt, s_yt.n), (s_g, s_g.n),
                                    (s_s, max(0, s_s.n - 1))] if cc == 0 else []))
                    c.do(SC, lambda e, o=o, pq=pq: e.activation(out=h1T[:, ts(o, 512)], in_=pq[:, 0:512],
                                                                func=AF.Gelu, bias=cb1_cs[:, o:o + 1],
                                                                scale=1.0),
                         inc=s_s, waits=[(s_mm, s_mm.n)])
                for o in range(4):
                    pq = PS_B if o % 2 == 0 else PS_C
                    for cc in range(8):
                        c.do(TE, lambda e, o=o, cc=cc, pq=pq: e.matmul(
                            pq[:, 0:512], cw2_s[:, cc * 512 + o * 128: cc * 512 + (o + 1) * 128],
                            h1T[:, ts(cc, 512)], start=(cc == 0), stop=(cc == 7)),
                            inc=(s_mm if cc == 7 else None),
                            waits=([(s_s, s_s.n), (s_s, max(0, s_s.n - 1))] if cc == 0 else []))
                    c.do(SC, lambda e, o=o, pq=pq: e.activation(out=h2T[:, ts(o, 512)], in_=pq[:, 0:512],
                                                                func=AF.Gelu, bias=cb2_cs[:, o:o + 1],
                                                                scale=1.0),
                         inc=s_s, waits=[(s_mm, s_mm.n)])
                for o in range(2):
                    pq = PS_B if o % 2 == 0 else PS_C
                    for cc in range(4):
                        c.do(TE, lambda e, o=o, cc=cc, pq=pq: e.matmul(
                            pq[:, 0:512], cw3_s[:, cc * 256 + o * 128: cc * 256 + (o + 1) * 128],
                            h2T[:, ts(cc, 512)], start=(cc == 0), stop=(cc == 3)),
                            inc=(s_mm if cc == 3 else None),
                            waits=([(s_s, s_s.n)] if cc == 0 else []))
                    c.do(SC, lambda e, o=o, pq=pq: e.activation(out=h3T[:, ts(o, 512)], in_=pq[:, 0:512],
                                                                func=AF.Gelu, bias=cb3_cs[:, o:o + 1],
                                                                scale=1.0),
                         inc=s_s, waits=[(s_mm, s_mm.n)])
                for cc in range(2):
                    c.do(TE, lambda e, q=q, cc=cc: e.matmul(
                        PS_A[0:1, 0:512], cw4_cs[:, cc:cc + 1], h3T[:, ts(cc, 512)],
                        start=(cc == 0), stop=(cc == 1)),
                        inc=(s_mm if cc == 1 else None),
                        waits=([(s_s, s_s.n)] if cc == 0 else []))
                c.do(VE, lambda e, q=q: e.tensor_single_scalar(tfq[:], PS_A[0:1, 0:512],
                                                               cb4_ss[0:1, 0:1], ALU.add),
                     inc=s_v, waits=[(s_mm, s_mm.n)])
                vfence()
                c.do(VE, lambda e: e.tensor_copy(tfq[:], tfq[:]), inc=s_v,
                     waits=[(s_v, s_v.n)])
                row2col(tfcol[:, q * 4: (q + 1) * 4], tfq[:], 512, waits=[(s_v, s_v.n)])
                tf_g[q] = s_g.n
            last = cyc == 2
            for q in range(4):
                c.do(SC, lambda e, q=q: e.activation(out=tfcol[:, q * 4: (q + 1) * 4],
                                                     in_=tfcol[:, q * 4: (q + 1) * 4],
                                                     func=AF.Sigmoid), inc=s_s,
                     waits=[(s_g, tf_g[q])])
                if DEBUG and cyc == 0 and q == 3:
                    c.do(SY, lambda e: e.dma_start(out=dbg["d_tf1"], in_=tfcol[:]), inc=s_st, inck=16,
                         waits=[(s_s, s_s.n)])
                c.do(VE, lambda e, q=q: e.tensor_scalar(ccol[:, q * 4: (q + 1) * 4],
                                                        tfcol[:, q * 4: (q + 1) * 4],
                                                        cBb[:, 0:1], cAb[:, 0:1],
                                                        ALU.mult, ALU.add), inc=s_v,
                     waits=[(s_s, s_s.n)])
                c.do(VE, lambda e, q=q: e.tensor_scalar(nccol[:, q * 4: (q + 1) * 4],
                                                        ccol[:, q * 4: (q + 1) * 4],
                                                        -1.0, None, ALU.mult), inc=s_v,
                     waits=[(s_v, s_v.n)])
                for t in range(q * 4, q * 4 + 4):
                    st0 = t * D
                    Q1 = state[:, st0: st0 + 256]
                    Q2 = state[:, st0 + 256: st0 + 512]
                    Q3 = state[:, st0 + 512: st0 + 768]
                    Q4 = state[:, st0 + 768: st0 + 1024]
                    c.do(VE, lambda e, Q1=Q1: e.tensor_copy(tmpq[:], Q1), inc=s_v)
                    c.do(VE, lambda e, t=t, Q1=Q1, Q3=Q3: e.scalar_tensor_tensor(
                        Q1, Q3, ccol[:, t:t + 1], Q1, ALU.mult, ALU.add), inc=s_v,
                        waits=[(s_v, s_v.n)])
                    c.do(VE, lambda e, t=t, Q3=Q3: e.scalar_tensor_tensor(
                        Q3, tmpq[:], nccol[:, t:t + 1], Q3, ALU.mult, ALU.add), inc=s_v,
                        waits=[(s_v, s_v.n)])
                    c.do(VE, lambda e, Q2=Q2: e.tensor_copy(tmpq[:], Q2), inc=s_v,
                         waits=[(s_v, s_v.n)])
                    c.do(VE, lambda e, t=t, Q2=Q2, Q4=Q4: e.scalar_tensor_tensor(
                        Q2, Q4, nccol[:, t:t + 1], Q2, ALU.mult, ALU.add), inc=s_v,
                        waits=[(s_v, s_v.n)])
                    c.do(VE, lambda e, t=t, Q4=Q4: e.scalar_tensor_tensor(
                        Q4, tmpq[:], ccol[:, t:t + 1], Q4, ALU.mult, ALU.add), inc=s_v,
                        waits=[(s_v, s_v.n)])
                    if last:
                        c.do(VE, lambda e, t=t: e.tensor_tensor(fin[:], x2[:, ts(t, D)],
                                                                state[:, ts(t, D)], ALU.add), inc=s_v,
                             waits=[(s_v, s_v.n)])
                        c.do(SY, lambda e, t=t: e.dma_start(out=out_own[ts(t, 128), :], in_=fin[:]),
                             inc=s_st, inck=16, waits=[(s_v, s_v.n)])
                        c.do(VE, lambda e: e.engine_nop(), waits=[(s_st, s_st.n)])
        if DEBUG and (DBGSEL >> 18) & 1:
            c.do(SY, lambda e: e.dma_start(out=dbg["d_state"], in_=state[:, 0:D]), inc=s_st, inck=16)

        if DEBUG and (DBGSEL >> 19) & 1:
            c.do(SY, lambda e: e.dma_start(out=dbg["d_M2"], in_=mmat[:]), inc=s_st, inck=16)
        c.do(SY, lambda e: e.nop(), waits=[(s_st, s_st.n)])
        c.barrier(sems["b9"])

    def reset_sems():
        for s in sems.values():
            s.n = 0

    def mkctx(eng, me, record=None):
        cx = Ctx(eng, me, record=record)
        cx.bar_buf = barbuf
        cx.bar_ps = PS_C[0:1, 511:512]
        cx.bar_w = onesb[0:1, 0:1]
        return cx

    if deadlock_check:
        traces = {}
        for me in ["sync", "scalar", "vector", "tensor", "gpsimd"]:
            reset_sems()
            rec = []
            prog(mkctx(None, me, record=rec))
            traces[me] = rec
        es.close()
        return traces

    with nc.Block() as block:
        @block.sync
        def _(eng):
            reset_sems()
            prog(mkctx(eng, "sync"))

        @block.scalar
        def _(eng):
            reset_sems()
            prog(mkctx(eng, "scalar"))

        @block.vector
        def _(eng):
            reset_sems()
            prog(mkctx(eng, "vector"))

        @block.tensor
        def _(eng):
            reset_sems()
            prog(mkctx(eng, "tensor"))

        @block.gpsimd
        def _(eng):
            reset_sems()
            prog(mkctx(eng, "gpsimd"))

    es.close()
    return nc


_NC_CACHE = {}


def _consts():
    e = {}
    e["ident_f"] = np.eye(128, dtype=np.float32)
    e["ident_b"] = np.eye(128).astype(ml_dtypes.bfloat16)
    e["iota2k"] = np.broadcast_to(np.arange(SK, dtype=np.float32), (128, SK)).copy()
    e["iota128r"] = np.broadcast_to(np.arange(128, dtype=np.float32), (128, 128)).copy()
    e["jconst"] = np.broadcast_to(np.arange(1, NSEL + 1, dtype=np.float32), (128, NSEL)).copy()
    e["i128c_bD"] = np.arange(128, dtype=np.float32).reshape(128, 1).astype(ml_dtypes.bfloat16)
    e["hi64_bD"] = np.broadcast_to(np.arange(NT, dtype=np.float32), (128, NT)).astype(ml_dtypes.bfloat16).copy()
    e["onesf_D"] = np.ones((128, 128), dtype=np.float32)
    e["onesb_D"] = np.ones((128, 128), dtype=ml_dtypes.bfloat16)
    return e


def kernel(L0_attn_x, L0_attn_Wq, L0_attn_bq, L0_attn_Wk, L0_attn_bk,
           L0_attn_Wv, L0_attn_bv, L0_attn_Wo, L0_attn_bo, L0_attn_sample_idx,
           L0_ffn_cn_w1, L0_ffn_cn_b1, L0_ffn_cn_w2, L0_ffn_cn_b2,
           L0_ffn_cn_w3, L0_ffn_cn_b3, L0_ffn_cn_w4, L0_ffn_cn_b4,
           L0_ffn_gc_w1, L0_ffn_gc_b1, L0_ffn_gc_w2, L0_ffn_gc_b2,
           L0_ffn_gc_w3, L0_ffn_gc_b3, L0_ffn_adaptive_range):
    x = np.asarray(L0_attn_x, dtype=np.float32)
    consts = _consts()
    if "nc" not in _NC_CACHE:
        _NC_CACHE["nc"] = build_nc()
    nc = _NC_CACHE["nc"]

    bf = ml_dtypes.bfloat16
    in_maps = []
    for c in range(8):
        b, qq = c // 4, c % 4
        hs = qq * 4 * HD
        m = dict(consts)
        m["xb"] = np.ascontiguousarray(x[b])
        m["xo"] = np.ascontiguousarray(x[b, qq * 2048:(qq + 1) * 2048])
        wq_sl = np.asarray(L0_attn_Wq, np.float32)[:, hs:hs + 256]
        wv_sl = np.asarray(L0_attn_Wv, np.float32)[:, hs:hs + 256]
        m["wvq"] = np.ascontiguousarray(np.concatenate([wv_sl, wq_sl], axis=1))
        m["wk"] = np.ascontiguousarray(np.asarray(L0_attn_Wk, np.float32)[:, hs:hs + 256])
        m["wo"] = np.ascontiguousarray(np.asarray(L0_attn_Wo, np.float32))
        bq = np.asarray(L0_attn_bq, np.float32)[hs:hs + 256]
        bk = np.asarray(L0_attn_bk, np.float32)[hs:hs + 256]
        bv = np.asarray(L0_attn_bv, np.float32)[hs:hs + 256]
        m["bvq_rbD"] = np.concatenate([bv, bq]).reshape(1, 512).astype(bf)
        m["bk_c"] = np.ascontiguousarray(bk.reshape(2, 128).T)
        m["bk_r"] = bk.reshape(1, 256)
        m["bo_r"] = np.asarray(L0_attn_bo, np.float32).reshape(1, D)
        m["sidx"] = np.ascontiguousarray(np.asarray(L0_attn_sample_idx, np.int32).reshape(16, 128).T)
        m["cw1"] = np.asarray(L0_ffn_cn_w1, np.float32)
        m["cw2"] = np.asarray(L0_ffn_cn_w2, np.float32)
        m["cw3"] = np.asarray(L0_ffn_cn_w3, np.float32)
        m["cw4_c"] = np.ascontiguousarray(np.asarray(L0_ffn_cn_w4, np.float32).reshape(2, 128).T)
        m["cb1_c"] = np.ascontiguousarray(np.asarray(L0_ffn_cn_b1, np.float32).reshape(8, 128).T)
        m["cb2_c"] = np.ascontiguousarray(np.asarray(L0_ffn_cn_b2, np.float32).reshape(4, 128).T)
        m["cb3_c"] = np.ascontiguousarray(np.asarray(L0_ffn_cn_b3, np.float32).reshape(2, 128).T)
        m["cb4_s"] = np.asarray(L0_ffn_cn_b4, np.float32).reshape(1, 1)
        m["gw1"] = np.asarray(L0_ffn_gc_w1, np.float32)
        m["gw2"] = np.asarray(L0_ffn_gc_w2, np.float32)
        m["gw3_c"] = np.ascontiguousarray(np.asarray(L0_ffn_gc_w3, np.float32).reshape(2, 128).T)
        m["gb1_r"] = np.asarray(L0_ffn_gc_b1, np.float32).reshape(1, 512)
        m["gb2_r"] = np.asarray(L0_ffn_gc_b2, np.float32).reshape(1, 256)
        m["gb3_s"] = np.asarray(L0_ffn_gc_b3, np.float32).reshape(1, 1)
        m["arng"] = np.asarray(L0_ffn_adaptive_range, np.float32).reshape(1, 1)
        m["tok0"] = np.asarray([[qq * 2048]], np.float32)
        in_maps.append(m)

    res = run_bass_kernel_spmd(nc, in_maps, core_ids=list(range(8)))
    kernel._last_res = res
    out = np.empty((B, S, D), dtype=np.float32)
    for c in range(8):
        b, qq = c // 4, c % 4
        out[b, qq * 2048:(qq + 1) * 2048] = res.results[c]["out_own"]
    return out


# version 08 (final)
# speedup vs baseline: 1.0323x; 1.0323x over previous
import numpy as np
import ml_dtypes
import concourse.bass as bass
import concourse.mybir as mybir
from concourse.bass_utils import run_bass_kernel_spmd

F32 = mybir.dt.float32
BF16 = mybir.dt.bfloat16
I32 = mybir.dt.int32
AF = mybir.ActivationFunctionType
ALU = mybir.AluOpType
AX = mybir.AxisListType.X

B, S, D, H = 2, 8192, 1024, 16
HD, HL = 64, 4
NT = 64
NTB = 16
NSEL = 48
SK = 2048
NKT = SK // 128
QW = 259

DEBUG = False
DBGSEL = 0x000000


class Sem:
    __slots__ = ("h", "n", "name")

    def __init__(self, h, name):
        self.h = h
        self.n = 0
        self.name = name


class Ctx:
    def __init__(self, eng, me, record=None):
        self.eng = eng
        self.me = me
        self.record = record
        self.srcline = None

    def do(self, engine, fn, inc=None, inck=1, waits=()):
        if self.record is not None:
            if self.me == engine:
                import traceback
                fr = traceback.extract_stack(limit=3)[0]
                self.record.append(([(s.name, v) for (s, v) in waits if v > 0],
                                    [(inc.name, inck)] if inc is not None else [],
                                    fr.lineno))
        elif self.me == engine:
            for (s, v) in waits:
                if v > 0:
                    self.eng.wait_ge(s.h, v)
            r = fn(self.eng)
            if inc is not None:
                if r is None:
                    raise RuntimeError("op returned None but inc requested")
                r.then_inc(inc.h, inck)
        if inc is not None:
            inc.n += inck

    def barrier(self, sem):
        if self.record is not None:
            sem.n += 5
            self.record.append(([], [(sem.name, 1)], 0))
            self.record.append(([(sem.name, sem.n)], [], 0))
            return
        if self.me == "sync":
            self.eng.sem_inc(sem.h, 1)
        elif self.me == "vector":
            self.eng.tensor_copy(self.bar_buf[0:1, 0:1], self.bar_buf[0:1, 1:2]).then_inc(sem.h, 1)
        elif self.me == "scalar":
            self.eng.copy(out=self.bar_buf[0:1, 2:3], in_=self.bar_buf[0:1, 3:4]).then_inc(sem.h, 1)
        elif self.me == "gpsimd":
            self.eng.memset(self.bar_buf[0:1, 4:5], 0.0).then_inc(sem.h, 1)
        elif self.me == "tensor":
            self.eng.matmul(self.bar_ps, self.bar_w, self.bar_w,
                            start=True, stop=True).then_inc(sem.h, 1)
        sem.n += 5
        self.eng.wait_ge(sem.h, sem.n)


def ts(i, n):
    return slice(i * n, (i + 1) * n)


class Alloc:
    def __init__(self, nc, base, top, tag):
        self.nc, self.base, self.top, self.tag = nc, base, top, tag
        self.cur = (base + 31) // 32 * 32

    def t(self, name, shape, dt):
        per = int(np.prod(shape[1:])) * mybir.dt.size(dt)
        per = (per + 31) // 32 * 32
        h = self.nc.alloc_sbuf_tensor_at(name, shape, dt, offset=self.cur)
        self.cur += per
        assert self.cur <= self.top, f"sbuf overflow [{self.tag}] at {name}: {self.cur} > {self.top}"
        return h


def build_nc(deadlock_check=False):
    nc = bass.Bass("TRN2", target_bir_lowering=False)

    I = lambda n, s, d=F32: nc.dram_tensor(n, s, d, kind="ExternalInput").ap()
    xb = I("xb", [S, D])
    xo = I("xo", [2048, D])
    wvq = I("wvq", [D, 512]); wk = I("wk", [D, 256])
    wo = I("wo", [D, D])
    bk_c = I("bk_c", [128, 2]); bk_r = I("bk_r", [1, 256])
    bo_r = I("bo_r", [1, D])
    bvq_rbD = I("bvq_rbD", [1, 512], BF16)
    sidx = I("sidx", [128, 16], I32)
    cw1 = I("cw1", [D, D]); cw2 = I("cw2", [D, 512]); cw3 = I("cw3", [512, 256])
    cw4_c = I("cw4_c", [128, 2])
    cb1_c = I("cb1_c", [128, 8]); cb2_c = I("cb2_c", [128, 4])
    cb3_c = I("cb3_c", [128, 2]); cb4_s = I("cb4_s", [1, 1])
    gw1 = I("gw1", [D, 512]); gw2 = I("gw2", [512, 256]); gw3_c = I("gw3_c", [128, 2])
    gb1_r = I("gb1_r", [1, 512]); gb2_r = I("gb2_r", [1, 256]); gb3_s = I("gb3_s", [1, 1])
    arng = I("arng", [1, 1])
    tok0 = I("tok0", [1, 1])
    ident_f = I("ident_f", [128, 128])
    ident_b = I("ident_b", [128, 128], BF16)
    iota2k = I("iota2k", [128, SK])
    iota128r = I("iota128r", [128, 128])
    jconst = I("jconst", [128, NSEL])
    i128c_bD = I("i128c_bD", [128, 1], BF16)
    hi64_bD = I("hi64_bD", [128, NT], BF16)
    onesf_D = I("onesf_D", [128, 128])
    onesb_D = I("onesb_D", [128, 128], BF16)

    out_own = nc.dram_tensor("out_own", [2048, D], F32, kind="ExternalOutput").ap()

    cc1_in = nc.dram_tensor("cc1_in", [128, 200], BF16)
    cc1_out = nc.dram_tensor("cc1_out", [512, 200], BF16)
    cc2_in = nc.dram_tensor("cc2_in", [NSEL, HL], F32)
    cc2_out = nc.dram_tensor("cc2_out", [4 * NSEL, HL], F32)
    cc0_in = nc.dram_tensor("cc0_in", [128, 32], F32)
    cc0_out = nc.dram_tensor("cc0_out", [512, 32], F32)
    cc3_in = nc.dram_tensor("cc3_in", [1, D], F32)
    row_bounce = nc.dram_tensor("row_bounce", [1, SK], F32)
    cand_bounce = nc.dram_tensor("cand_bounce", [128, 4 * 8], F32)
    cc3_out = nc.dram_tensor("cc3_out", [4, D], F32)

    dbg = {}
    if DEBUG:
        for nm, shape in [("d_M", [128, 256]), ("d_M2", [128, 256]), ("d_qid", [48, HL]),
                          ("d_kt", [128, 128]), ("d_qt", [128, 259]),
                          ("d_k0m", [128, 256]), ("d_mmax", [128, 256]),
                          ("d_cand", [4, 1024]), ("d_tops", [4, 48]),
                          ("d_rank", [128, 64]), ("d_maskM", [128, 64]),
                          ("d_payall", [128, 800]), ("d_wob", [64, 1024]), ("d_corr", [128, 1024]),
                          ("d_esc", [128, 128]), ("d_qsh", [128, 8]), ("d_corr6", [128, 1024]),
                          ("d_et", [128, 48]), ("d_extr", [48, 3]),
                          ("d_mmean", [128, 256]), ("d_mask", [128, 256]),
                          ("d_pay", [128, 200]), ("d_kbar", [1, 256]),
                          ("d_cnt", [1, SK]), ("d_thr", [1, HL]),
                          ("d_x2", [128, D]), ("d_gf", [1, 1]),
                          ("d_zcs", [1, D]), ("d_gcq", [1, D]), ("d_tf1", [128, NTB]),
                          ("d_grow1", [1, 512]), ("d_grow2", [1, 256]),
                          ("d_corr6b", [128, D]), ("d_qtz", [128, 128]),
                          ("d_rb", [1, D]), ("d_state", [128, D])]:
            dbg[nm] = nc.dram_tensor(nm, shape, F32, kind="ExternalOutput").ap()

    perm = Alloc(nc, nc.sbuf_base, nc.sbuf_top, "perm")
    id_f = perm.t("id_f", [128, 128], F32)
    id_b = perm.t("id_b", [128, 128], BF16)
    onesf = perm.t("onesf", [128, 128], F32)
    onesb = perm.t("onesb", [128, 128], BF16)
    iota128r_s = perm.t("iota128r_s", [128, 128], F32)
    jconst_s = perm.t("jconst_s", [128, NSEL], F32)
    one1f = perm.t("one1f", [1, 1], F32)
    vfb = perm.t("vfb", [128, 16], F32)
    barbuf = perm.t("barbuf", [1, 8], F32)
    zrow = perm.t("zrow", [1, 128], F32)
    payload = perm.t("payload", [128, 200], BF16)
    pay_all = perm.t("pay_all", [128, 800], BF16)
    qid_c = perm.t("qid_c", [48, HL], F32)
    qid_all = perm.t("qid_all", [16, NSEL], F32)
    tok0_s = perm.t("tok0_s", [1, 1], F32)
    tok0b = perm.t("tok0b", [128, 1], F32)
    arng_s = perm.t("arng_s", [1, 1], F32)
    constv = {}
    for val in (1e-5, 0.125, 1.0 / S):
        t_ = perm.t(f"cst_{len(constv)}", [128, 1], F32)
        nc.const_aps.register(F32, val, t_[:])
        constv[val] = t_
    pstart = perm.cur

    A = Alloc(nc, pstart, nc.sbuf_top, "A")
    wvq_s = A.t("wvq_s", [128, 8 * 512], BF16)
    wk_s = A.t("wk_s", [128, 2048], BF16)
    bk_cs = A.t("bk_cs", [128, 2], F32)
    bk_rs = A.t("bk_rs", [1, 256], F32)
    bvq_rb = A.t("bvq_rb", [1, 512], BF16)
    i128c_b = A.t("i128c_b", [128, 1], BF16)
    hi64_b = A.t("hi64_b", [128, NT], BF16)
    iota2k_s = A.t("iota2k_s", [128, SK], F32)
    xsum = A.t("xsum", [128, NT], F32)
    xsq = A.t("xsq", [128, NT], F32)
    xsum_l = A.t("xsum_l", [128, 16], F32)
    xsq_l = A.t("xsq_l", [128, 16], F32)
    rstd = A.t("rstd", [128, NT], F32)
    mrs = A.t("mrs", [128, NT], F32)
    xt = [A.t(f"xt{i}", [128, D], F32) for i in range(2)]
    yt = [A.t(f"yt{i}", [128, D], BF16) for i in range(2)]
    yT4 = [A.t(f"yT4{i}", [128, 4096], BF16) for i in range(2)]
    kT = A.t("kT", [128, 2 * S], BF16)
    k0m = A.t("k0m", [128, 2 * SK], BF16)
    vtok = A.t("vtok", [128, HL * NT * 65], BF16)
    qtok = A.t("qtok", [128, NT * QW], BF16)
    sidx_s = A.t("sidx_s", [128, 16], I32)
    sidx_f = A.t("sidx_f", [128, 16], F32)
    etile = A.t("etile", [128, SK], BF16)
    cnt_row = A.t("cnt_row", [1, SK], F32)
    cnt_col_b = A.t("cnt_col_b", [128, 16], BF16)
    mask_row = A.t("mask_row", [1, SK], F32)
    maskb = A.t("maskb", [128, SK], BF16)
    ybw_row = A.t("ybw_row", [1, D], F32)
    ybw_col_b = A.t("ybw_col_b", [128, 8], BF16)
    kbar_row = A.t("kbar_row", [1, 256], F32)
    kbar_col = A.t("kbar_col", [128, 2], F32)
    kbar_col_b = A.t("kbar_col_b", [128, 2], BF16)
    qTz = [A.t(f"qTz{i}", [128, 128], BF16) for i in range(4)]
    mst = A.t("mst", [128, 4], F32)
    mmax = A.t("mmax", [128, HL * NT], F32)
    mmean = A.t("mmean", [128, HL * NT], F32)
    mmat = A.t("mmat", [128, HL * NT], F32)
    mwork = A.t("mwork", [128, HL * NT], F32)
    cand = A.t("cand", [128, 8], F32)
    candf4 = A.t("candf4", [HL, 1024], F32)
    g84 = A.t("g84", [HL, 8], F32)
    d4 = A.t("d4", [HL, HL], F32)
    topsq = A.t("topsq", [HL, NSEL], F32)
    thr_c = A.t("thr_c", [128, HL], F32)
    maskM = A.t("maskM", [128, NT], F32)
    zer64 = A.t("zer64", [128, NT], F32)
    rankM = A.t("rankM", [128, NT], F32)
    rtot = A.t("rtot", [1, 128], F32)
    rsc = A.t("rsc", [1, 128], F32)
    roff_c = A.t("roff_c", [128, 1], F32)
    et_s = [A.t(f"et_s{i}", [128, NSEL], BF16) for i in range(2)]
    qsel_r = A.t("qsel_r", [48, 64], F32)
    qselT = A.t("qselT", [128, NSEL], BF16)
    qsel_st = A.t("qsel_st", [64, NSEL], BF16)
    extr = A.t("extr", [48, 3], F32)
    pcol = A.t("pcol", [48, 1], F32)
    padrow = A.t("padrow", [1, NSEL], F32)
    m2acc = A.t("m2acc", [128, NT], F32)
    m2fin = A.t("m2fin", [1, 1], F32)
    m2b = A.t("m2b", [128, 1], F32)
    expT = [A.t(f"expT{i}", [128, NSEL + 1], BF16) for i in range(2)]
    ctxTs = A.t("ctxTs", [128, NSEL + 1], F32)
    recipr = A.t("recipr", [128, NSEL], F32)
    recb = A.t("recb", [64, NSEL], F32)
    padb = A.t("padb", [64, NSEL], F32)
    vm_c = A.t("vm_c", [64, 1], F32)
    t48a = A.t("t48a", [64, NSEL], F32)
    t48b = A.t("t48b", [64, NSEL], F32)
    a_end = A.cur

    Bv = Alloc(nc, pstart, nc.sbuf_top, "B1")
    x2 = Bv.t("x2", [128, NTB * D], F32)
    state = Bv.t("state", [128, NTB * D], F32)
    bo_s = Bv.t("bo_s", [1, D], F32)
    rb_row = Bv.t("rb_row", [1, D], F32)
    rb_row_b = Bv.t("rb_row_b", [1, D], BF16)
    qidcol8 = Bv.t("qidcol8", [128, 8], F32)
    qsh8 = Bv.t("qsh8", [128, 8], F32)
    x2sum = Bv.t("x2sum", [128, NTB], F32)
    x2sq = Bv.t("x2sq", [128, NTB], F32)
    rstd2 = Bv.t("rstd2", [128, NTB], F32)
    mrs2 = Bv.t("mrs2", [128, NTB], F32)
    gf_s = Bv.t("gf_s", [1, 1], F32)
    tmp1s = Bv.t("tmp1s", [1, 1], F32)
    cA = Bv.t("cA", [1, 1], F32)
    cB = Bv.t("cB", [1, 1], F32)
    cAb = Bv.t("cAb", [128, 1], F32)
    cBb = Bv.t("cBb", [128, 1], F32)
    gb1_rs = Bv.t("gb1_rs", [1, 512], F32)
    gb2_rs = Bv.t("gb2_rs", [1, 256], F32)
    gb3_ss = Bv.t("gb3_ss", [1, 1], F32)
    gw3_cs = Bv.t("gw3_cs", [128, 2], BF16)
    cw4_cs = Bv.t("cw4_cs", [128, 2], BF16)
    cb1_cs = Bv.t("cb1_cs", [128, 8], F32)
    cb2_cs = Bv.t("cb2_cs", [128, 4], F32)
    cb3_cs = Bv.t("cb3_cs", [128, 2], F32)
    cb4_ss = Bv.t("cb4_ss", [1, 1], F32)
    tfq = Bv.t("tfq", [1, 512], F32)
    tfcol = Bv.t("tfcol", [128, NTB], F32)
    ccol = Bv.t("ccol", [128, NTB], F32)
    nccol = Bv.t("nccol", [128, NTB], F32)
    tmpq = Bv.t("tmpq", [128, 256], F32)
    b2start = Bv.cur
    wo_b = Bv.t("wo_b", [64, 8 * D], BF16)
    corr = Bv.t("corr", [128, 8 * D], BF16)
    xtb = [Bv.t(f"xtb{i}", [128, D], F32) for i in range(2)]
    esc = [Bv.t(f"esc{i}", [128, 128], BF16) for i in range(2)]
    corr_st = Bv.t("corr_st", [48, 8 * D], BF16)
    C2 = Alloc(nc, b2start, nc.sbuf_top, "C2")
    gw1_s = C2.t("gw1_s", [128, 8 * 512], BF16)
    gw2_s = C2.t("gw2_s", [128, 4 * 256], BF16)
    zcs = C2.t("zcs", [1, D], F32)
    gcrow = C2.t("gcrow", [4, D], F32)
    gcq = C2.t("gcq", [1, D], F32)
    gc_col = C2.t("gc_col", [128, 8], BF16)
    grow1 = C2.t("grow1", [1, 512], F32)
    gcol1 = C2.t("gcol1", [128, 4], BF16)
    grow2 = C2.t("grow2", [1, 256], F32)
    gcol2 = C2.t("gcol2", [128, 2], BF16)
    Cv = Alloc(nc, b2start, nc.sbuf_top, "C")
    cw1_s = Cv.t("cw1_s", [128, 8 * D], BF16)
    cw2_s = Cv.t("cw2_s", [128, 8 * 512], BF16)
    cw3_s = Cv.t("cw3_s", [128, 4 * 256], BF16)
    sbuf_b = Cv.t("sbuf_b", [128, D], BF16)
    zb = Cv.t("zb", [128, D], BF16)
    sT = Cv.t("sT", [128, 4096], BF16)
    h1T = Cv.t("h1T", [128, 8 * 512], BF16)
    h2T = Cv.t("h2T", [128, 4 * 512], BF16)
    h3T = Cv.t("h3T", [128, 2 * 512], BF16)
    fin = Cv.t("fin", [128, D], F32)
    b_end = max(Bv.cur, Cv.cur)

    from contextlib import ExitStack
    es = ExitStack()
    PS_A = es.enter_context(nc.psum_tensor("PS_A", [128, 2048], F32))
    PS_B = es.enter_context(nc.psum_tensor("PS_B", [128, 512], F32))
    PS_C = es.enter_context(nc.psum_tensor("PS_C", [128, 512], F32))
    PS_T = es.enter_context(nc.psum_tensor("PS_T", [128, 1024], BF16))
    PS_T2 = es.enter_context(nc.psum_tensor("PS_T2", [128, 1024], BF16))

    sems = {}
    for name in ["s_x", "s_x2", "s_yt", "s_mm", "s_ev", "s_v", "s_s", "s_g",
                 "s_cc", "s_st", "s_ld", "s_tp"] + [f"b{i}" for i in range(10)]:
        sems[name] = Sem(es.enter_context(nc.semaphore(name)), name)

    def prog(c: Ctx):
        s_x, s_yt, s_tp = sems["s_x"], sems["s_yt"], sems["s_tp"]
        s_x2 = sems["s_x2"]
        s_mm, s_ev, s_v = sems["s_mm"], sems["s_ev"], sems["s_v"]
        s_s, s_g, s_cc = sems["s_s"], sems["s_g"], sems["s_cc"]
        s_st, s_ld = sems["s_st"], sems["s_ld"]
        SY, SC, VE, TE, GP = "sync", "scalar", "vector", "tensor", "gpsimd"

        def vfence():
            c.do(VE, lambda e: e.tensor_scalar_add(vfb[:], vfb[:], 0.0))
            c.do(VE, lambda e: e.tensor_scalar_add(vfb[:], vfb[:], 0.0))

        def row2col(dst_ap, src_ap, n, p=128, waits=()):
            c.do(GP, lambda e: e.dma_start(out=row_bounce.ap()[0:1, 0:n], in_=src_ap),
                 inc=s_g, inck=16, waits=list(waits))
            c.do(GP, lambda e: e.dma_start(
                out=dst_ap, in_=row_bounce.ap()[0:1, 0:n].rearrange("o (t j) -> (o j) t", j=p),
                allow_slow_non_contiguous=True),
                inc=s_g, inck=16, waits=[(s_g, s_g.n)])

        for dst, src in [(wvq_s, wvq), (wk_s, wk)]:
            c.do(GP, lambda e, dst=dst, src=src: e.dma_start(
                out=dst[:], in_=src.rearrange("(c p) j -> p c j", p=128)),
                inc=s_g, inck=16)
        for dst, src in [(bk_cs, bk_c), (bk_rs, bk_r), (bvq_rb, bvq_rbD),
                         (id_f, ident_f), (id_b, ident_b),
                         (onesf, onesf_D), (onesb, onesb_D), (iota2k_s, iota2k),
                         (iota128r_s, iota128r), (jconst_s, jconst),
                         (i128c_b, i128c_bD), (hi64_b, hi64_bD), (sidx_s, sidx),
                         (tok0_s, tok0), (arng_s, arng)]:
            c.do(SY, lambda e, dst=dst, src=src: e.dma_start(out=dst[:], in_=src),
                 inc=s_ld, inck=16)
        nload = s_ld.n
        ngp0 = s_g.n
        for val, t_ in constv.items():
            c.do(VE, lambda e, t_=t_, val=val: e.memset(t_[:], float(val)))
        c.do(VE, lambda e: e.memset(one1f[:], 1.0))
        c.do(VE, lambda e: e.memset(vfb[:], 0.0))
        c.do(VE, lambda e: e.memset(zrow[:], 0.0))
        c.do(VE, lambda e: e.memset(zer64[:], 0.0), inc=s_v)
        c.do(VE, lambda e: e.tensor_copy(sidx_f[:], sidx_s[:]), inc=s_v, waits=[(s_ld, nload)])

        for t in range(16):
            buf = t % 2
            eng_q, sem_q = (SY, s_x) if t % 2 == 0 else (SC, s_x2)
            c.do(eng_q, lambda e, t=t, buf=buf: e.dma_start(out=xt[buf][:], in_=xo[ts(t, 128), :]),
                 inc=sem_q, inck=16,
                 waits=([(s_v, s_v.n), (s_s, s_s.n)] if t >= 2 else []))
            c.do(VE, lambda e, t=t, buf=buf: e.reduce_sum(xsum_l[:, t:t + 1], xt[buf][:], axis=AX),
                 inc=s_v, waits=[(sem_q, sem_q.n)])
            c.do(SC, lambda e, t=t, buf=buf: e.activation(out=yt[buf][:], in_=xt[buf][:], func=AF.Square,
                                                          accum_out=xsq_l[:, t:t + 1]),
                 inc=s_s, waits=[(sem_q, sem_q.n)])
        vfence()
        c.do(VE, lambda e: e.tensor_copy(xt[1][:, 0:16], xsum_l[:]), inc=s_v)
        c.do(SC, lambda e: e.copy(out=barbuf[0:1, 6:7], in_=barbuf[0:1, 7:8]))
        c.do(SC, lambda e: e.copy(out=xt[0][:, 0:16], in_=xsq_l[:]), inc=s_s)
        c.do(GP, lambda e: e.dma_start(out=cc0_in.ap()[:, 0:16], in_=xsum_l[:]), inc=s_g, inck=16,
             waits=[(s_v, s_v.n)])
        c.do(GP, lambda e: e.dma_start(out=cc0_in.ap()[:, 16:32], in_=xsq_l[:]), inc=s_g, inck=16,
             waits=[(s_s, s_s.n)])
        c.do(GP, lambda e: e.collective_compute(
            "AllGather", ALU.bypass, replica_groups=[[0, 1, 2, 3], [4, 5, 6, 7]],
            ins=[cc0_in.ap().opt()], outs=[cc0_out.ap().opt()]), inc=s_cc,
            waits=[(s_g, s_g.n)])
        cc0R = cc0_out.ap().rearrange("(r p) w -> p r w", p=128)
        c.do(GP, lambda e: e.dma_start(out=xsum[:].rearrange("p (r t) -> p r t", r=4),
                                       in_=cc0R[:, :, 0:16]), inc=s_g, inck=16,
             waits=[(s_cc, s_cc.n)])
        c.do(GP, lambda e: e.dma_start(out=xsq[:].rearrange("p (r t) -> p r t", r=4),
                                       in_=cc0R[:, :, 16:32]), inc=s_g, inck=16)
        nstat = s_g.n

        for t in range(NKT):
            c.do(VE, lambda e, t=t: e.tensor_single_scalar(etile[:], iota2k_s[:], sidx_f[:, t:t + 1], ALU.is_equal),
                 inc=s_v, waits=([(s_mm, s_mm.n)] if t >= 1 else []))
            for kk in range(4):
                c.do(TE, lambda e, t=t, kk=kk: e.matmul(PS_A[0:1, ts(kk, 512)], onesb[:, 0:1], etile[:, ts(kk, 512)],
                                                        start=(t == 0), stop=(t == NKT - 1)),
                     inc=(s_mm if kk == 3 else None),
                     waits=([(s_v, s_v.n), (s_ld, nload)] if kk == 0 else []))
        c.do(VE, lambda e: e.tensor_copy(cnt_row[:], PS_A[0:1, :]), waits=[(s_mm, s_mm.n)])
        c.do(VE, lambda e: e.tensor_scalar(mask_row[:], cnt_row[:], 0.5, None, ALU.is_gt), inc=s_v)
        row2col(cnt_col_b[:], cnt_row[:], SK, waits=[(s_v, s_v.n)])
        for kk in range(4):
            c.do(TE, lambda e, kk=kk: e.matmul(PS_A[:, ts(kk, 512)], onesf[0:1, :], mask_row[0:1, ts(kk, 512)],
                                               start=True, stop=True),
                 inc=(s_mm if kk == 3 else None), waits=([(s_v, s_v.n)] if kk == 0 else []))
        c.do(VE, lambda e: e.tensor_copy(maskb[:], PS_A[:, 0:SK]), inc=s_v, waits=[(s_mm, s_mm.n)])

        c.do(VE, lambda e: e.tensor_scalar(rstd[:], xsum[:], 1.0 / D, None, ALU.mult),
             inc=s_v, waits=[(s_g, nstat)])
        c.do(VE, lambda e: e.tensor_tensor(mrs[:], rstd[:], rstd[:], ALU.mult), inc=s_v,
             waits=[(s_v, s_v.n)])
        c.do(VE, lambda e: e.scalar_tensor_tensor(mrs[:], xsq[:], 1.0 / D, mrs[:], ALU.mult, ALU.subtract),
             inc=s_v, waits=[(s_v, s_v.n)])
        c.do(SC, lambda e: e.activation(out=mrs[:], in_=mrs[:], func=AF.Sqrt, bias=1e-5, scale=1.0),
             inc=s_s, waits=[(s_v, s_v.n)])
        c.do(VE, lambda e: e.tensor_scalar(xsum[:], xsum[:], 1.0 / D, None, ALU.mult),
             inc=s_v, waits=[(s_s, s_s.n)])
        c.do(VE, lambda e: e.reciprocal(rstd[:], mrs[:]), inc=s_v, waits=[(s_v, s_v.n)])
        c.do(VE, lambda e: e.tensor_tensor(mrs[:], xsum[:], rstd[:], ALU.mult), inc=s_v,
             waits=[(s_v, s_v.n)])
        c.do(VE, lambda e: e.tensor_scalar(mrs[:], mrs[:], -1.0, None, ALU.mult), inc=s_v,
             waits=[(s_v, s_v.n)])

        c.barrier(sems["b0"])

        ev_base = s_ev.n
        evn_t = {}
        evn_g = {}
        for g in range(NT // 4):
            gbuf = g % 2
            for ti in range(4):
                t = g * 4 + ti
                buf = t % 2
                pst = PS_T if t % 2 == 0 else PS_T2
                c.do(SY, lambda e, t=t, buf=buf: e.dma_start(out=xt[buf][:], in_=xb[ts(t, 128), :]),
                     inc=s_x, inck=16, waits=([(s_s, max(0, s_s.n - 1))] if t >= 2 else []))
                c.do(VE, lambda e, t=t, buf=buf: e.tensor_scalar(yt[buf][:], xt[buf][:],
                                                                 rstd[:, t:t + 1], mrs[:, t:t + 1],
                                                                 ALU.mult, ALU.add),
                     inc=s_s, waits=[(s_x, s_x.n)] + ([(s_tp, s_tp.n)] if t >= 2 else []))
                for cc in range(8):
                    c.do(TE, lambda e, cc=cc, buf=buf, pst=pst: e.transpose(
                        pst[:, ts(cc, 128)], yt[buf][:, ts(cc, 128)], id_b[:]),
                        inc=(s_tp if cc == 7 else None),
                        waits=([(s_s, s_s.n), (s_yt, max(0, s_yt.n - 1))] if cc == 0 else []))
                if t < NKT:
                    for half, pq in ((0, PS_A[0:1, 0:512]), (1, PS_A[0:1, 512:1024])):
                        c.do(TE, lambda e, t=t, buf=buf, half=half, pq=pq: e.matmul(
                            pq, cnt_col_b[:, t:t + 1], yt[buf][:, ts(half, 512)],
                            start=(t == 0), stop=(t == NKT - 1)),
                            inc=(s_mm if (t == NKT - 1 and half == 1) else None),
                            waits=([(s_g, s_g.n)] if (t == 0 and half == 0) else []))
                for cc in range(8):
                    c.do(VE, lambda e, cc=cc, g=g, ti=ti, gbuf=gbuf, pst=pst: e.tensor_copy(
                        yT4[gbuf][:, cc * 512 + ti * 128: cc * 512 + (ti + 1) * 128],
                        pst[:, ts(cc, 128)]),
                        inc=(s_yt if cc == 7 else None),
                        waits=([(s_tp, s_tp.n)] if cc == 0 else []))
            wy = [(s_yt, s_yt.n), (s_g, ngp0)]
            for o, pq in ((0, PS_B), (1, PS_C)):
                for cc in range(8):
                    c.do(TE, lambda e, o=o, cc=cc, gbuf=gbuf, pq=pq: e.matmul(
                        pq[:, 0:512], wk_s[:, cc * 256 + o * 128: cc * 256 + (o + 1) * 128],
                        yT4[gbuf][:, ts(cc, 512)], start=(cc == 0), stop=(cc == 7)),
                        inc=(s_mm if cc == 7 else None),
                        waits=((wy + [(s_ev, evn_g.get(g - 1, 0))]) if cc == 0 else []))
            c.do(SC, lambda e, g=g: e.activation(out=kT[:, ts(g, 512)], in_=PS_B[:, 0:512],
                                                 func=AF.Identity, bias=bk_cs[:, 0:1], scale=1.0),
                 waits=[(s_mm, s_mm.n)])
            c.do(SC, lambda e, g=g: e.activation(out=kT[:, S + g * 512: S + (g + 1) * 512],
                                                 in_=PS_C[:, 0:512],
                                                 func=AF.Identity, bias=bk_cs[:, 1:2], scale=1.0),
                 inc=s_ev)
            evn_g[g] = s_ev.n
            for ti in range(4):
                t = g * 4 + ti
                pb = PS_A[:, 1024:1536] if t % 2 == 0 else PS_A[:, 1536:2048]
                for cc in range(8):
                    c.do(TE, lambda e, cc=cc, gbuf=gbuf, ti=ti, pb=pb: e.matmul(
                        pb, yT4[gbuf][:, cc * 512 + ti * 128: cc * 512 + (ti + 1) * 128],
                        wvq_s[:, ts(cc, 512)], start=(cc == 0), stop=False),
                        waits=([(s_ev, evn_t.get(t - 2, 0))] if cc == 0 else []))
                c.do(TE, lambda e, pb=pb: e.matmul(pb, onesb[0:1, :], bvq_rb[:],
                                                   start=False, stop=True), inc=s_mm)
                c.do(SC, lambda e, t=t, pb=pb: e.copy(
                    out=vtok[:].rearrange("p (h t w) -> p h t w", h=HL, t=NT)[:, :, t, 0:64],
                    in_=pb[:, 0:256].rearrange("p (h d) -> p h d", h=HL)),
                    waits=[(s_mm, s_mm.n)])
                c.do(SC, lambda e, t=t, pb=pb: e.copy(out=qtok[:, t * QW: t * QW + 256],
                                                      in_=pb[:, 256:512]),
                     inc=s_ev)
                evn_t[t] = s_ev.n
        qv = qtok[:].rearrange("p (t w) -> p t w", w=QW)
        c.do(VE, lambda e: e.tensor_copy(qv[:, :, 256:257], onesb[:, 0:1].to_broadcast([128, NT, 1])),
             waits=[(s_ev, s_ev.n)])
        c.do(VE, lambda e: e.tensor_copy(qv[:, :, 257:258], i128c_b[:, 0:1].to_broadcast([128, NT, 1])))
        c.do(VE, lambda e: e.tensor_copy(qv[:, :, 258:259], hi64_b[:].rearrange('p t -> p t ()')), inc=s_v)
        vvv = vtok[:].rearrange("p (u w) -> p u w", w=65)
        c.do(VE, lambda e: e.tensor_copy(vvv[:, :, 64:65], onesb[:, 0:1].to_broadcast([128, HL * NT, 1])), inc=s_v)

        c.do(VE, lambda e: e.tensor_copy(ybw_row[:, 0:512], PS_A[0:1, 0:512]), waits=[(s_mm, s_mm.n)])
        c.do(VE, lambda e: e.tensor_copy(ybw_row[:, 512:1024], PS_A[0:1, 512:1024]), inc=s_v)
        row2col(ybw_col_b[:], ybw_row[:], D, waits=[(s_v, s_v.n), (s_yt, s_yt.n)])
        for cc in range(8):
            c.do(TE, lambda e, cc=cc: e.matmul(PS_B[0:1, 0:256], ybw_col_b[:, cc:cc + 1], wk_s[:, ts(cc, 256)],
                                               start=(cc == 0), stop=(cc == 7)),
                 inc=(s_mm if cc == 7 else None),
                 waits=([(s_g, s_g.n), (s_s, s_s.n)] if cc == 0 else []))
        c.do(VE, lambda e: e.scalar_tensor_tensor(kbar_row[:], PS_B[0:1, 0:256], 1.0 / SK, bk_rs[:],
                                                  ALU.mult, ALU.add), inc=s_v, waits=[(s_mm, s_mm.n)])
        vfence()
        c.do(VE, lambda e: e.tensor_copy(kbar_row[:], kbar_row[:]), inc=s_v,
             waits=[(s_v, s_v.n)])
        row2col(kbar_col[:], kbar_row[:], 256, waits=[(s_v, s_v.n)])
        c.do(VE, lambda e: e.tensor_copy(kbar_col_b[:], kbar_col[:]), inc=s_v, waits=[(s_g, s_g.n)])

        for p in range(2):
            for hf in range(2):
                c.do(VE, lambda e, p=p, hf=hf: e.tensor_single_scalar(
                    xt[0][:], kT[:, p * S + hf * 1024: p * S + (hf + 1) * 1024],
                    kbar_col[:, p:p + 1], ALU.subtract), inc=s_v)
                c.do(VE, lambda e, p=p, hf=hf: e.tensor_tensor(
                    xt[0][:], xt[0][:], maskb[:, ts(hf, 1024)], ALU.mult), inc=s_v,
                     waits=[(s_v, s_v.n)])
                c.do(VE, lambda e, p=p, hf=hf: e.tensor_single_scalar(
                    k0m[:, p * SK + hf * 1024: p * SK + (hf + 1) * 1024], xt[0][:],
                    kbar_col[:, p:p + 1], ALU.add), inc=s_v, waits=[(s_v, s_v.n)])

        c.barrier(sems["b1"])
        if DEBUG and (DBGSEL >> 1) & 1:
            c.do(SY, lambda e: e.dma_start(out=dbg["d_cnt"], in_=cnt_row[:]), inc=s_st, inck=16)
            c.do(SY, lambda e: e.dma_start(out=dbg["d_kbar"], in_=kbar_row[:]), inc=s_st, inck=16)

        for h in range(HL):
            c.do(VE, lambda e, h=h: e.memset(qTz[h][:], 0.0), inc=(s_v if h == 3 else None))
        ve_ck = []
        mm_ck = {}
        sc_rel = {}
        mst_ck = 0
        ci = 0
        for t in range(NT):
            for p in range(2):
                pst = PS_T if p == 0 else PS_T2
                c.do(TE, lambda e, t=t, p=p, pst=pst: e.transpose(
                    pst[:, 0:128],
                    qtok[:, t * QW + p * 128: t * QW + (p + 1) * 128], id_b[:]),
                    inc=(s_tp if p == 1 else None),
                    waits=([(s_s, s_s.n), (s_ev, s_ev.n)] if p == 0 else []))
            for h in range(HL):
                p, par = h // 2, h % 2
                pst = PS_T if p == 0 else PS_T2
                c.do(SC, lambda e, h=h, par=par, pst=pst: e.copy(
                    out=qTz[h][par * 64: par * 64 + 64, :],
                    in_=pst[par * 64: par * 64 + 64, 0:128]),
                    inc=(s_ev if h == 3 else None),
                    waits=([(s_tp, s_tp.n), (s_v, s_v.n if t == 0 else 0)] if h == 0 else [])
                    + ([(s_mm, mm_ck.get((t - 1, h), 0))]))
            for h in range(HL):
                p = h // 2
                for j in range(2):
                    pi = ci % 2
                    if ci == 0:
                        wrel = [(s_v, s_v.n)]
                    elif ci >= 2:
                        wrel = [ve_ck[ci - 2]]
                    else:
                        wrel = []
                    for kk in range(2):
                        c.do(TE, lambda e, h=h, p=p, j=j, kk=kk, pi=pi: e.matmul(
                            PS_A[:, pi * 1024 + kk * 512: pi * 1024 + (kk + 1) * 512],
                            qTz[h][:], k0m[:, p * SK + j * 1024 + kk * 512: p * SK + j * 1024 + (kk + 1) * 512],
                            start=True, stop=True),
                            inc=(s_mm if (j == 1 and kk == 1) else None),
                            waits=(([(s_ev, s_ev.n)] + wrel) if kk == 0 else []))
                    if j == 0:
                        c.do(TE, lambda e, h=h, p=p: e.matmul(
                            PS_B[:, 0:1], qTz[h][:], kbar_col_b[:, p:p + 1],
                            start=True, stop=True), inc=s_mm)
                        c.do(VE, lambda e, t=t, h=h: e.tensor_copy(
                            mmean[:, h * NT + t: h * NT + t + 1], PS_B[:, 0:1]),
                            waits=[(s_mm, s_mm.n)])
                        c.do(VE, lambda e, pi=pi, h=h: e.tensor_scalar(
                            etile[:, 0:1024], PS_A[:, pi * 1024: (pi + 1) * 1024], 0.0, None,
                            ALU.add, ALU.max, accum_out=mst[:, h: h + 1]),
                            inc=s_v)
                        ve_ck.append((s_v, s_v.n))
                        mst_ck = s_v.n
                    else:
                        mm_ck[(t, h)] = s_mm.n
                        c.do(VE, lambda e: e.tensor_scalar_add(vfb[:], vfb[:], 0.0),
                             waits=[(s_v, mst_ck)])
                        c.do(VE, lambda e: e.tensor_scalar_add(vfb[:], vfb[:], 0.0))
                        c.do(VE, lambda e, pi=pi, t=t, h=h: e.tensor_scalar(
                            etile[:, 0:1024], PS_A[:, pi * 1024: (pi + 1) * 1024],
                            mst[:, h: h + 1], None,
                            ALU.max, ALU.max, accum_out=mmax[:, h * NT + t: h * NT + t + 1]),
                            inc=s_v, waits=[(s_mm, s_mm.n)])
                        ve_ck.append((s_v, s_v.n))
                    ci += 1
        vfence()
        c.do(VE, lambda e: e.tensor_tensor(mmat[:], mmax[:], mmean[:], ALU.subtract), inc=s_v,
             waits=[(s_v, s_v.n)])
        if DEBUG and (DBGSEL >> 2) & 1:
            c.do(GP, lambda e: e.dma_start(out=dbg["d_qtz"], in_=qTz[0][:]), inc=s_st, inck=16,
                 waits=[(s_mm, s_mm.n)])

        c.barrier(sems["b2"])
        if DEBUG and (DBGSEL >> 3) & 1:
            c.do(SY, lambda e: e.dma_start(out=dbg["d_M"], in_=mmat[:]), inc=s_st, inck=16)
            c.do(SY, lambda e: e.nop(), waits=[(s_st, s_st.n)])
            c.do(GP, lambda e: e.dma_start(out=dbg["d_kt"], in_=kT[:, 0:128]), inc=s_st, inck=16)
            c.do(GP, lambda e: e.dma_start(out=dbg["d_qt"], in_=qtok[:, 0:QW]), inc=s_st, inck=16)
            c.do(GP, lambda e: e.dma_start(out=dbg["d_k0m"], in_=k0m[:, 0:256]), inc=s_st, inck=16)
            c.do(SY, lambda e: e.dma_start(out=dbg["d_mmax"], in_=mmax[:]), inc=s_st, inck=16)
            c.do(SY, lambda e: e.dma_start(out=dbg["d_mmean"], in_=mmean[:]), inc=s_st, inck=16)
            c.do(GP, lambda e: e.dma_start(out=dbg["d_mask"], in_=maskb[:, 0:256]), inc=s_st, inck=16)

        for h in range(HL):
            c.do(VE, lambda e, h=h: e.max(out=mwork[:, h * 8: h * 8 + 8], in_=mmat[:, ts(h, NT)]),
                 inc=s_v, waits=[(s_v, s_v.n)])
        vfence()
        c.do(VE, lambda e: e.tensor_copy(etile[:, 0:32], mwork[:, 0:32]), inc=s_v,
             waits=[(s_v, s_v.n)])
        c.do(GP, lambda e: e.dma_start(out=cand_bounce.ap(), in_=mwork[:, 0:32]), inc=s_g, inck=16,
             waits=[(s_v, s_v.n)])
        for h in range(HL):
            c.do(GP, lambda e, h=h: e.dma_start(out=candf4[h:h + 1, :],
                                                in_=cand_bounce.ap()[:, h * 8:(h + 1) * 8]),
                 inc=s_g, inck=16, waits=([(s_g, s_g.n)] if h == 0 else []))
        if DEBUG and (DBGSEL >> 4) & 1:
            c.do(SY, lambda e: e.dma_start(out=dbg["d_cand"], in_=candf4[:]), inc=s_st, inck=16,
                 waits=[(s_g, s_g.n)])
            c.do(SY, lambda e: e.nop(), waits=[(s_st, s_st.n)])
        for r in range(6):
            c.do(VE, lambda e, r=r: e.max(out=topsq[:, r * 8: (r + 1) * 8], in_=candf4[:]),
                 inc=s_v, waits=[(s_v, s_v.n)] + ([(s_g, s_g.n)] if r == 0 else []))
            if r < 5:
                c.do(VE, lambda e, r=r: e.match_replace(out=candf4[:],
                                                        in_to_replace=topsq[:, r * 8: (r + 1) * 8],
                                                        in_values=candf4[:], imm_value=-3e38),
                     inc=s_v, waits=[(s_v, s_v.n)])
        if DEBUG and (DBGSEL >> 5) & 1:
            c.do(SY, lambda e: e.dma_start(out=dbg["d_tops"], in_=topsq[:]), inc=s_st, inck=16,
                 waits=[(s_v, s_v.n)])
            c.do(SY, lambda e: e.nop(), waits=[(s_st, s_st.n)])
        c.do(VE, lambda e: e.tensor_single_scalar(d4[:], id_f[0:HL, 0:HL], topsq[:, 44:45], ALU.mult),
             inc=s_v, waits=[(s_v, s_v.n)])
        c.do(TE, lambda e: e.matmul(PS_B[:, 0:HL], onesf[0:HL, :], d4[:], start=True, stop=True),
             inc=s_mm, waits=[(s_v, s_v.n)])
        c.do(VE, lambda e: e.tensor_copy(thr_c[:, 0:HL], PS_B[:, 0:HL]), inc=s_v,
             waits=[(s_mm, s_mm.n)])
        if DEBUG and (DBGSEL >> 6) & 1:
            c.do(SY, lambda e: e.dma_start(out=dbg["d_thr"], in_=thr_c[0:1, 0:HL]), inc=s_st, inck=16,
                 waits=[(s_v, s_v.n)])

        c.do(VE, lambda e: e.memset(payload[:], 0.0), inc=s_v)
        c.do(VE, lambda e: e.memset(qselT[:], 0.0), inc=s_v)
        for h in range(HL):
            par, p = h % 2, h // 2
            sl = slice(par * 64, par * 64 + 64)
            c.do(VE, lambda e, h=h: e.tensor_single_scalar(maskM[:], mmat[:, ts(h, NT)], thr_c[:, h:h + 1],
                                                           ALU.is_ge), inc=s_v, waits=[(s_v, s_v.n)])
            c.do(VE, lambda e: e.tensor_tensor_scan(rankM[:], maskM[:], zer64[:], 0.0, ALU.add, ALU.add),
                 inc=s_v, waits=[(s_v, s_v.n)])
            c.do(TE, lambda e: e.transpose(PS_B[0:1, 0:128], rankM[:, NT - 1: NT], id_f[:]),
                 inc=s_mm, waits=[(s_v, s_v.n)])
            c.do(VE, lambda e: e.tensor_copy(rtot[:], PS_B[0:1, 0:128]), inc=s_v, waits=[(s_mm, s_mm.n)])
            c.do(VE, lambda e: e.tensor_tensor_scan(rsc[:], rtot[:], zrow[:], 0.0, ALU.add, ALU.add),
                 inc=s_v)
            c.do(VE, lambda e: e.tensor_tensor(rsc[:], rsc[:], rtot[:], ALU.subtract), inc=s_v,
                 waits=[(s_v, s_v.n)])
            c.do(TE, lambda e: e.transpose(PS_B[:, 0:1], rsc[:], one1f[:]), inc=s_mm, waits=[(s_v, s_v.n)])
            c.do(VE, lambda e: e.tensor_copy(roff_c[:], PS_B[:, 0:1]), inc=s_v, waits=[(s_mm, s_mm.n)])
            c.do(VE, lambda e: e.tensor_single_scalar(rankM[:], rankM[:], roff_c[:, 0:1], ALU.add),
                 inc=s_v, waits=[(s_v, s_v.n)])
            if DEBUG and h == 0 and (DBGSEL >> 21) & 1:
                c.do(SY, lambda e: e.dma_start(out=dbg["d_rank"], in_=rankM[:]), inc=s_st, inck=16,
                     waits=[(s_v, s_v.n)])
                c.do(SY, lambda e: e.dma_start(out=dbg["d_maskM"], in_=maskM[:]), inc=s_st, inck=16)
                c.do(SY, lambda e: e.nop(), waits=[(s_st, s_st.n)])
            vbase = s_v.n
            for t in range(NT):
                eb = t % 2
                c.do(VE, lambda e, t=t, eb=eb: e.tensor_scalar(et_s[eb][:], jconst_s[:], rankM[:, t:t + 1],
                                                               maskM[:, t:t + 1], ALU.is_equal, ALU.mult),
                     inc=s_v, waits=([(s_mm, s_mm.n)] if t >= 2 else [(s_v, s_v.n)]))
                c.do(TE, lambda e, t=t, h=h, eb=eb: e.matmul(
                    PS_C[0:48, 0:64], et_s[eb][:], qtok[:, t * QW + h * 64: t * QW + (h + 1) * 64],
                    start=(t == 0), stop=(t == NT - 1)),
                    waits=([(s_v, vbase + t + 1)] + ([(s_s, s_s.n)] if t == 0 else [])))
                c.do(TE, lambda e, t=t, eb=eb: e.matmul(
                    PS_B[0:48, 0:3], et_s[eb][:], qtok[:, t * QW + 256: t * QW + 259],
                    start=(t == 0), stop=(t == NT - 1)),
                    inc=s_mm)
            c.do(VE, lambda e: e.tensor_copy(qsel_r[:], PS_C[0:48, 0:64]), waits=[(s_mm, s_mm.n)])
            c.do(VE, lambda e: e.tensor_copy(extr[:], PS_B[0:48, 0:3]), inc=s_v)
            if DEBUG and h == 0 and (DBGSEL >> 22) & 1:
                c.do(GP, lambda e: e.dma_start(out=dbg["d_et"], in_=et_s[1][:]), inc=s_st, inck=16,
                     waits=[(s_v, s_v.n)])
                c.do(SY, lambda e: e.dma_start(out=dbg["d_extr"], in_=extr[:]), inc=s_st, inck=16,
                     waits=[(s_v, s_v.n)])
                c.do(SY, lambda e: e.nop(), waits=[(s_st, s_st.n)])
            c.do(TE, lambda e: e.transpose(PS_C[0:64, 0:48], qsel_r[:], id_f[0:48, 0:48]),
                 inc=s_mm, waits=[(s_v, s_v.n)])
            if par == 0:
                c.do(SC, lambda e: e.copy(out=qselT[0:64, :], in_=PS_C[0:64, 0:48]), inc=s_s,
                     waits=[(s_mm, s_mm.n)])
            else:
                c.do(SC, lambda e: e.copy(out=qsel_st[:], in_=PS_C[0:64, 0:48]),
                     waits=[(s_mm, s_mm.n)])
                c.do(SC, lambda e: e.copy(out=barbuf[0:1, 6:7], in_=barbuf[0:1, 7:8]))
                c.do(SC, lambda e: e.copy(out=qsel_st[:], in_=qsel_st[:]), inc=s_s)
                c.do(GP, lambda e: e.dma_start(out=qselT[64:128, :], in_=qsel_st[:]),
                     inc=s_g, inck=16, waits=[(s_s, s_s.n)])
                c.do(SC, lambda e: e.copy(out=zer64[0:1, 0:1], in_=zer64[0:1, 0:1]), inc=s_s,
                     waits=[(s_g, s_g.n)])
            c.do(VE, lambda e: e.scalar_tensor_tensor(pcol[:], extr[:, 2:3], 128.0, extr[:, 1:2],
                                                      ALU.mult, ALU.add), inc=s_v,
                 waits=[(s_v, s_v.n)])
            c.do(VE, lambda e, h=h: e.scalar_tensor_tensor(qid_c[:, h:h + 1], extr[:, 0:1], -1e6, pcol[:],
                                                           ALU.mult, ALU.add), inc=s_v,
                 waits=[(s_v, s_v.n)])
            c.do(VE, lambda e, h=h: e.tensor_scalar(qid_c[:, h:h + 1], qid_c[:, h:h + 1], 1e6, None, ALU.add),
                 inc=s_v, waits=[(s_v, s_v.n)])
            c.do(TE, lambda e: e.transpose(PS_B[0:1, 0:48], extr[:, 0:1], id_f[0:48, 0:48]),
                 inc=s_mm, waits=[(s_v, s_v.n)])
            c.do(VE, lambda e: e.tensor_copy(padrow[:], PS_B[0:1, 0:48]), inc=s_v, waits=[(s_mm, s_mm.n)])
            vb2 = s_v.n
            sb2 = s_s.n
            for u in range(NT):
                psl = PS_A[:, 0:48] if u % 2 == 0 else PS_A[:, 512:560]
                c.do(TE, lambda e, u=u, p=p, sl=sl, psl=psl: e.matmul(
                    psl, kT[sl, p * S + u * 128: p * S + (u + 1) * 128], qselT[sl, :],
                    start=True, stop=True), inc=s_mm,
                    waits=([(s_s, s_s.n), (s_v, s_v.n)] if u == 0 else [(s_v, vb2 + u - 1)]))
                c.do(VE, lambda e, u=u, psl=psl: e.tensor_scalar(
                    etile[:, 0:48], psl, 0.0, None, ALU.add, ALU.max,
                    accum_out=m2acc[:, u:u + 1]), inc=s_v, waits=[(s_mm, s_mm.n)])
            vfence()
            c.do(VE, lambda e: e.reduce_max(m2b[:, 0:1], m2acc[:], axis=AX), inc=s_v,
                 waits=[(s_v, s_v.n)])
            c.do(TE, lambda e: e.transpose(PS_B[0:1, 0:128], m2b[:], id_f[:]), inc=s_mm,
                 waits=[(s_v, s_v.n)])
            c.do(VE, lambda e: e.reduce_max(m2fin[:], PS_B[0:1, 0:128], axis=AX), inc=s_v,
                 waits=[(s_mm, s_mm.n)])
            vfence()
            c.do(VE, lambda e: e.tensor_scalar(m2fin[:], m2fin[:], -0.125, None, ALU.mult), inc=s_v,
                 waits=[(s_v, s_v.n)])
            vfence()
            c.do(VE, lambda e: e.tensor_copy(m2fin[:], m2fin[:]), inc=s_v,
                 waits=[(s_v, s_v.n)])
            c.do(TE, lambda e: e.matmul(PS_B[:, 0:1], onesf[0:1, :], m2fin[:], start=True, stop=True),
                 inc=s_mm, waits=[(s_v, s_v.n)])
            c.do(VE, lambda e: e.tensor_copy(m2b[:], PS_B[:, 0:1]), inc=s_v, waits=[(s_mm, s_mm.n)])
            c.do(VE, lambda e: e.memset(expT[0][:], 0.0))
            c.do(VE, lambda e: e.memset(expT[1][:], 0.0))
            c.do(VE, lambda e: e.tensor_copy(expT[0][:, 48:49], onesb[:, 0:1]))
            c.do(VE, lambda e: e.tensor_copy(expT[1][:, 48:49], onesb[:, 0:1]), inc=s_v)
            sb3 = s_s.n
            mmb3 = s_mm.n
            for u in range(NT):
                pu = u % 2
                psl = PS_A[:, 0:48] if pu == 0 else PS_A[:, 512:560]
                c.do(TE, lambda e, u=u, p=p, sl=sl, psl=psl: e.matmul(
                    psl, kT[sl, p * S + u * 128: p * S + (u + 1) * 128], qselT[sl, :],
                    start=True, stop=True), inc=s_mm,
                    waits=([(s_v, s_v.n)] if u == 0 else [(s_s, sb3 + u - 1)]))
                c.do(SC, lambda e, u=u, pu=pu, psl=psl: e.activation(
                    out=expT[pu][:, 0:48], in_=psl, func=AF.Exp, bias=m2b[:, 0:1], scale=0.125),
                    inc=s_s, waits=[(s_mm, mmb3 + 2 * u + 1), (s_mm, max(mmb3, mmb3 + 2 * (u - 2) + 2))])
                c.do(TE, lambda e, u=u, h=h, pu=pu: e.matmul(
                    PS_C[0:65, 0:49], vtok[:, (h * NT + u) * 65: (h * NT + u) * 65 + 65], expT[pu][:],
                    start=(u == 0), stop=(u == NT - 1)), inc=s_mm,
                    waits=[(s_s, sb3 + u + 1)])
            c.do(VE, lambda e: e.tensor_copy(ctxTs[0:65, :], PS_C[0:65, 0:49]), inc=s_v,
                 waits=[(s_mm, s_mm.n)])
            c.do(VE, lambda e: e.reciprocal(recipr[64:65, :], ctxTs[64:65, 0:48]), inc=s_v)
            c.do(VE, lambda e: e.tensor_scalar_add(vfb[:], vfb[:], 0.0), waits=[(s_v, s_v.n)])
            c.do(VE, lambda e: e.tensor_copy(recipr[64:65, :], recipr[64:65, :]), inc=s_v)
            c.do(GP, lambda e: e.dma_start(out=recipr[0:1, :], in_=recipr[64:65, :]), inc=s_g, inck=16,
                 waits=[(s_v, s_v.n)])
            c.do(VE, lambda e: e.tensor_tensor(recipr[0:1, :], recipr[0:1, :], padrow[:], ALU.mult),
                 inc=s_v, waits=[(s_g, s_g.n)])
            c.do(TE, lambda e: e.matmul(PS_B[0:64, 0:48], onesf[0:1, 0:64], recipr[0:1, :],
                                        start=True, stop=True), waits=[(s_v, s_v.n)])
            c.do(TE, lambda e: e.matmul(PS_B[0:64, 64:112], onesf[0:1, 0:64], padrow[:],
                                        start=True, stop=True), inc=s_mm)
            c.do(VE, lambda e: e.tensor_copy(recb[:], PS_B[0:64, 0:48]), waits=[(s_mm, s_mm.n)])
            c.do(VE, lambda e: e.tensor_copy(padb[:], PS_B[0:64, 64:112]), inc=s_v)
            c.do(VE, lambda e: e.tensor_scalar(vm_c[:], ctxTs[0:64, 48:49], 1.0 / S, None, ALU.mult),
                 inc=s_v)
            c.do(VE, lambda e: e.tensor_tensor(t48a[:], ctxTs[0:64, 0:48], recb[:], ALU.mult))
            c.do(VE, lambda e: e.tensor_single_scalar(t48b[:], padb[:], vm_c[:, 0:1], ALU.mult),
                 inc=s_v, waits=[(s_v, s_v.n)])
            c.do(VE, lambda e: e.tensor_tensor(t48a[:], t48a[:], t48b[:], ALU.subtract), inc=s_v,
                 waits=[(s_v, s_v.n)])
            c.do(VE, lambda e, h=h: e.tensor_copy(payload[0:64, h * 50: h * 50 + 48], t48a[:]))
            c.do(VE, lambda e, h=h: e.tensor_copy(payload[0:64, h * 50 + 48: h * 50 + 49], vm_c[:]),
                 inc=s_v)

        c.barrier(sems["b3"])
        if DEBUG and (DBGSEL >> 7) & 1:
            c.do(SY, lambda e: e.dma_start(out=dbg["d_qid"], in_=qid_c[:]), inc=s_st, inck=16)
            c.do(VE, lambda e: e.tensor_copy(mwork[:, 0:200], payload[:]), inc=s_v)
            c.do(SY, lambda e: e.dma_start(out=dbg["d_pay"], in_=mwork[:, 0:200]), inc=s_st, inck=16,
                 waits=[(s_v, s_v.n)])

        c.do(GP, lambda e: e.dma_start(out=cc1_in.ap(), in_=payload[:]), inc=s_g, inck=16)
        c.do(GP, lambda e: e.dma_start(out=cc2_in.ap(), in_=qid_c[:]), inc=s_g, inck=16)
        c.do(GP, lambda e: e.collective_compute(
            "AllGather", ALU.bypass, replica_groups=[[0, 1, 2, 3], [4, 5, 6, 7]],
            ins=[cc1_in.ap().opt()], outs=[cc1_out.ap().opt()]), inc=s_cc,
            waits=[(s_g, s_g.n)])
        c.do(GP, lambda e: e.collective_compute(
            "AllGather", ALU.bypass, replica_groups=[[0, 1, 2, 3], [4, 5, 6, 7]],
            ins=[cc2_in.ap().opt()], outs=[cc2_out.ap().opt()]), inc=s_cc)
        c.do(GP, lambda e: e.dma_start(out=pay_all[:].rearrange("p (r w) -> p r w", r=4),
                                       in_=cc1_out.ap().rearrange("(r p) w -> p r w", p=128)),
             inc=s_g, inck=16, waits=[(s_cc, s_cc.n)])

        c.barrier(sems["b4"])

        if DEBUG and (DBGSEL >> 8) & 1:
            c.do(GP, lambda e: e.dma_start(out=dbg["d_payall"], in_=pay_all[:]), inc=s_st, inck=16)
            c.do(GP, lambda e: e.dma_start(out=dbg["d_payall"], in_=pay_all[:]) if False else e.dma_start(
                out=cc1_in.ap(), in_=payload[:]), inc=s_st, inck=16) if False else None
        c.do(SY, lambda e: e.dma_start(out=bo_s[:], in_=bo_r), inc=s_ld, inck=16)
        c.do(VE, lambda e: e.memset(qidcol8[:], 5e6), inc=s_v)
        for a in range(2):
            for peer in range(4):
                c.do(GP, lambda e, a=a, peer=peer: e.dma_start(
                    out=qidcol8[a * 64: a * 64 + 48, peer * 2: peer * 2 + 2],
                    in_=cc2_out.ap()[peer * 48: peer * 48 + 48, a::2],
                    allow_slow_non_contiguous=True),
                    inc=s_g, inck=16, waits=([(s_v, s_v.n)] if a == 0 and peer == 0 else []))
        c.do(TE, lambda e: e.matmul(PS_B[:, 0:1], onesf[0:1, :], tok0_s[:], start=True, stop=True),
             inc=s_mm, waits=[(s_ld, s_ld.n)])
        c.do(VE, lambda e: e.tensor_copy(tok0b[:], PS_B[:, 0:1]), inc=s_v, waits=[(s_mm, s_mm.n)])
        c.do(VE, lambda e: e.tensor_single_scalar(qsh8[:], qidcol8[:], tok0b[:, 0:1], ALU.subtract),
             inc=s_v, waits=[(s_g, s_g.n), (s_v, s_v.n)])

        c.do(VE, lambda e: e.memset(corr[:], 0.0), inc=s_v)
        woR = wo.rearrange("(g p) j -> p g j", p=64)
        for half in range(2):
            c.do(GP, lambda e, half=half: e.dma_start(
                out=wo_b[:], in_=woR[:, half * 8: (half + 1) * 8, :]),
                inc=s_g, inck=16,
                waits=([(s_mm, s_mm.n), (s_ev, s_ev.n)] if half == 1 else []))
            for gl in range(8):
                g = half * 8 + gl
                peer, hh = g // 4, g % 4
                pslice = pay_all[0:64, peer * 200 + hh * 50: peer * 200 + hh * 50 + 48]
                vslice = pay_all[0:64, peer * 200 + hh * 50 + 48: peer * 200 + hh * 50 + 49]
                for kk in range(2):
                    c.do(TE, lambda e, gl=gl, g=g, kk=kk, vslice=vslice: e.matmul(
                        PS_A[0:1, 0:512] if kk == 0 else PS_A[0:1, 512:1024],
                        vslice, wo_b[:, gl * D + kk * 512: gl * D + (kk + 1) * 512],
                        start=(g == 0), stop=(g == 15)),
                        waits=([(s_g, s_g.n)] if (gl == 0 and kk == 0) else []))
                for kk in range(2):
                    c.do(TE, lambda e, gl=gl, g=g, kk=kk, pslice=pslice: e.matmul(
                        PS_A[0:48, 1024 + kk * 512: 1024 + (kk + 1) * 512],
                        pslice, wo_b[:, gl * D + kk * 512: gl * D + (kk + 1) * 512],
                        start=True, stop=True),
                        inc=(s_mm if kk == 1 else None),
                        waits=([(s_ev, s_ev.n), (s_v, s_v.n),
                                (s_g, s_g.n)] if kk == 0 else []))
                if g % 2 == 0:
                    c.do(SC, lambda e, g=g: e.copy(
                        out=corr[0:48, (g // 2) * D: (g // 2 + 1) * D], in_=PS_A[0:48, 1024:2048]),
                        inc=s_ev, waits=[(s_mm, s_mm.n)])
                else:
                    c.do(SC, lambda e, g=g: e.copy(
                        out=corr_st[:, (g // 2) * D: (g // 2 + 1) * D], in_=PS_A[0:48, 1024:2048]),
                        waits=[(s_mm, s_mm.n)])
                    c.do(SC, lambda e: e.copy(out=barbuf[0:1, 6:7], in_=barbuf[0:1, 7:8]))
                    c.do(SC, lambda e, g=g: e.copy(
                        out=corr_st[:, (g // 2) * D: (g // 2 + 1) * D],
                        in_=corr_st[:, (g // 2) * D: (g // 2 + 1) * D]), inc=s_ev)
                    c.do(GP, lambda e, g=g: e.dma_start(
                        out=corr[64:112, (g // 2) * D: (g // 2 + 1) * D],
                        in_=corr_st[:, (g // 2) * D: (g // 2 + 1) * D]),
                        inc=s_g, inck=16, waits=[(s_ev, s_ev.n)])
        if DEBUG and (DBGSEL >> 9) & 1:
            c.do(GP, lambda e: e.dma_start(out=dbg["d_wob"], in_=wo_b[:, 0:1024]), inc=s_st, inck=16)
            c.do(GP, lambda e: e.dma_start(out=dbg["d_corr"], in_=corr[:, 0:1024]), inc=s_st, inck=16)
            c.do(GP, lambda e: e.dma_start(out=cc1_in.ap()[0:1, 0:1], in_=onesb[0:1, 0:1]),
                 inc=s_st, inck=16)
            c.do(SY, lambda e: e.nop(), waits=[(s_st, s_st.n)])
        c.do(VE, lambda e: e.tensor_copy(rb_row[:, 0:512], PS_A[0:1, 0:512]),
             waits=[(s_mm, s_mm.n), (s_ev, s_ev.n)])
        c.do(VE, lambda e: e.tensor_copy(rb_row[:, 512:1024], PS_A[0:1, 512:1024]))
        c.do(VE, lambda e: e.tensor_tensor(rb_row[:], rb_row[:], bo_s[:], ALU.add),
             waits=[(s_ld, s_ld.n)])
        c.do(VE, lambda e: e.tensor_copy(rb_row_b[:], rb_row[:]), inc=s_v)
        if DEBUG and (DBGSEL >> 10) & 1:
            c.do(SY, lambda e: e.dma_start(out=dbg["d_rb"], in_=rb_row[:]), inc=s_st, inck=16,
                 waits=[(s_v, s_v.n)])


        ev_b = s_ev.n
        if DEBUG and (DBGSEL >> 11) & 1:
            c.do(GP, lambda e: e.dma_start(out=dbg["d_qsh"], in_=qsh8[:]), inc=s_st, inck=16,
                 waits=[(s_v, s_v.n)])
            c.do(GP, lambda e: e.dma_start(out=dbg["d_corr6"], in_=corr[:, 6 * D:7 * D]),
                 inc=s_st, inck=16)
            c.do(GP, lambda e: e.dma_start(out=dbg["d_corr6b"], in_=corr[:, 6 * D:7 * D]),
                 inc=s_st, inck=16, waits=[(s_g, s_g.n), (s_ev, s_ev.n)])
            c.do(SY, lambda e: e.nop(), waits=[(s_st, s_st.n)])
        for t in range(NTB):
            for cc in range(8):
                eb = cc % 2
                c.do(VE, lambda e, t=t, cc=cc, eb=eb: e.tensor_scalar(
                    esc[eb][:], iota128r_s[:], float(128 * t), qsh8[:, cc:cc + 1], ALU.add, ALU.is_equal),
                    inc=s_v, waits=([(s_mm, s_mm.n)] if cc >= 2 or t > 0 else []))
                if DEBUG and t == 0 and cc == 6 and (DBGSEL >> 23) & 1:
                    c.do(GP, lambda e, eb=eb: e.dma_start(out=dbg["d_esc"], in_=esc[eb][:]),
                         inc=s_st, inck=16, waits=[(s_v, s_v.n)])
                    c.do(SY, lambda e: e.nop(), waits=[(s_st, s_st.n)])
                for kk in range(2):
                    c.do(TE, lambda e, t=t, cc=cc, eb=eb, kk=kk: e.matmul(
                        PS_A[:, kk * 512: (kk + 1) * 512], esc[eb][:],
                        corr[:, cc * D + kk * 512: cc * D + (kk + 1) * 512],
                        start=(cc == 0), stop=False),
                        inc=(s_mm if kk == 1 else None),
                        waits=([(s_v, s_v.n), (s_ev, s_ev.n), (s_g, s_g.n)] if kk == 0 else []))
            for kk in range(2):
                c.do(TE, lambda e, kk=kk: e.matmul(
                    PS_A[:, kk * 512: (kk + 1) * 512], onesb[0:1, :], rb_row_b[:, kk * 512: (kk + 1) * 512],
                    start=False, stop=True), inc=(s_mm if kk == 1 else None))
            c.do(SY, lambda e, t=t: e.dma_start(out=xtb[t % 2][:], in_=xo[ts(t, 128), :]),
                 inc=s_x, inck=16, waits=([(s_v, s_v.n)] if t >= 2 else []))
            c.do(VE, lambda e, t=t: e.tensor_tensor(x2[:, ts(t, D)], xtb[t % 2][:], PS_A[:, 0:1024], ALU.add),
                 inc=s_v, waits=[(s_x, s_x.n), (s_mm, s_mm.n)])
        if DEBUG and (DBGSEL >> 12) & 1:
            c.do(SY, lambda e: e.dma_start(out=dbg["d_x2"], in_=x2[:, 0:D]), inc=s_st, inck=16,
                 waits=[(s_v, s_v.n)])

        for t in range(NTB):
            c.do(VE, lambda e, t=t: e.reduce_sum(x2sum[:, t:t + 1], x2[:, ts(t, D)], axis=AX), inc=s_v)
            c.do(SC, lambda e, t=t: e.activation(out=zb[:], in_=x2[:, ts(t, D)], func=AF.Square,
                                                 accum_out=x2sq[:, t:t + 1]), inc=s_s,
                 waits=[(s_v, s_v.n)] if t == 0 else [])
        c.do(VE, lambda e: e.tensor_scalar(rstd2[:], x2sum[:], 1.0 / D, None, ALU.mult),
             inc=s_v, waits=[(s_v, s_v.n), (s_s, s_s.n)])
        c.do(VE, lambda e: e.tensor_tensor(mrs2[:], rstd2[:], rstd2[:], ALU.mult), inc=s_v,
             waits=[(s_v, s_v.n)])
        c.do(VE, lambda e: e.scalar_tensor_tensor(mrs2[:], x2sq[:], 1.0 / D, mrs2[:], ALU.mult, ALU.subtract),
             inc=s_v, waits=[(s_v, s_v.n)])
        c.do(SC, lambda e: e.activation(out=mrs2[:], in_=mrs2[:], func=AF.Sqrt, bias=1e-5, scale=1.0),
             inc=s_s, waits=[(s_v, s_v.n)])
        c.do(VE, lambda e: e.tensor_scalar(x2sum[:], x2sum[:], 1.0 / D, None, ALU.mult),
             inc=s_v, waits=[(s_s, s_s.n)])
        c.do(VE, lambda e: e.reciprocal(rstd2[:], mrs2[:]), inc=s_v, waits=[(s_v, s_v.n)])
        c.do(VE, lambda e: e.tensor_tensor(mrs2[:], x2sum[:], rstd2[:], ALU.mult), inc=s_v,
             waits=[(s_v, s_v.n)])
        c.do(VE, lambda e: e.tensor_scalar(mrs2[:], mrs2[:], -1.0, None, ALU.mult), inc=s_v,
             waits=[(s_v, s_v.n)])
        for dst, src in [(gw1_s, gw1), (gw2_s, gw2)]:
            c.do(GP, lambda e, dst=dst, src=src: e.dma_start(
                out=dst[:], in_=src.rearrange("(c p) j -> p c j", p=128)), inc=s_g, inck=16,
                waits=[(s_mm, s_mm.n), (s_ev, s_ev.n), (s_v, s_v.n)])
        c.do(GP, lambda e: e.dma_start(out=gw3_cs[:], in_=gw3_c), inc=s_g, inck=16)
        c.do(GP, lambda e: e.dma_start(out=cw4_cs[:], in_=cw4_c), inc=s_g, inck=16)
        for dst, src in [(cb1_cs, cb1_c), (cb2_cs, cb2_c), (cb3_cs, cb3_c), (cb4_ss, cb4_s),
                         (gb1_rs, gb1_r), (gb2_rs, gb2_r), (gb3_ss, gb3_s)]:
            c.do(SY, lambda e, dst=dst, src=src: e.dma_start(out=dst[:], in_=src), inc=s_ld, inck=16)
        nload2 = s_ld.n
        for t in range(NTB):
            c.do(SC, lambda e, t=t: e.activation(out=zb[:], in_=x2[:, ts(t, D)], func=AF.Identity,
                                                 bias=mrs2[:, t:t + 1], scale=rstd2[:, t:t + 1]),
                 inc=s_s, waits=[(s_mm, s_mm.n), (s_v, s_v.n)] if t == 0 else [(s_mm, s_mm.n)])
            c.do(VE, lambda e, t=t: e.tensor_scalar(state[:, ts(t, D)], x2[:, ts(t, D)],
                                                    rstd2[:, t:t + 1], mrs2[:, t:t + 1],
                                                    ALU.mult, ALU.add), inc=s_v,
                 waits=([(s_v, s_v.n)] if t == 0 else []))
            for half, pq in ((0, PS_A[0:1, 0:512]), (1, PS_A[0:1, 512:1024])):
                c.do(TE, lambda e, t=t, half=half, pq=pq: e.matmul(
                    pq, onesb[:, 0:1], zb[:, ts(half, 512)],
                    start=(t == 0), stop=(t == NTB - 1)),
                    inc=(s_mm if half == 1 else None),
                    waits=([(s_s, s_s.n)] if half == 0 else []))
        c.do(VE, lambda e: e.tensor_copy(zcs[:, 0:512], PS_A[0:1, 0:512]), waits=[(s_mm, s_mm.n)])
        c.do(VE, lambda e: e.tensor_copy(zcs[:, 512:1024], PS_A[0:1, 512:1024]), inc=s_v)
        if DEBUG and (DBGSEL >> 13) & 1:
            c.do(SY, lambda e: e.dma_start(out=dbg["d_zcs"], in_=zcs[:]), inc=s_st, inck=16,
                 waits=[(s_v, s_v.n)])

        c.do(GP, lambda e: e.dma_start(out=cc3_in.ap(), in_=zcs[:]), inc=s_g, inck=16,
             waits=[(s_v, s_v.n)])
        c.do(GP, lambda e: e.collective_compute(
            "AllGather", ALU.bypass, replica_groups=[[0, 1, 2, 3], [4, 5, 6, 7]],
            ins=[cc3_in.ap().opt()], outs=[cc3_out.ap().opt()]), inc=s_cc,
            waits=[(s_g, s_g.n)])
        c.do(GP, lambda e: e.dma_start(out=gcrow[:], in_=cc3_out.ap()), inc=s_g, inck=16,
             waits=[(s_cc, s_cc.n)])
        for kk in range(2):
            c.do(TE, lambda e, kk=kk: e.matmul(PS_A[0:1, 0:512] if kk == 0 else PS_A[0:1, 512:1024],
                                               onesf[0:4, 0:1], gcrow[:, kk * 512: (kk + 1) * 512],
                                               start=True, stop=True),
                 inc=(s_mm if kk == 1 else None), waits=([(s_g, s_g.n)] if kk == 0 else []))
        c.do(SC, lambda e: e.activation(out=gcq[:, 0:512], in_=PS_A[0:1, 0:512], func=AF.Identity,
                                        scale=constv[1.0 / S][0:1, 0:1]), waits=[(s_mm, s_mm.n)])
        c.do(SC, lambda e: e.activation(out=gcq[:, 512:1024], in_=PS_A[0:1, 512:1024], func=AF.Identity,
                                        scale=constv[1.0 / S][0:1, 0:1]))
        c.do(SC, lambda e: e.copy(out=barbuf[0:1, 6:7], in_=barbuf[0:1, 7:8]))
        c.do(SC, lambda e: e.copy(out=gcq[:], in_=gcq[:]), inc=s_s)
        if DEBUG and (DBGSEL >> 14) & 1:
            c.do(SY, lambda e: e.dma_start(out=dbg["d_gcq"], in_=gcq[:]), inc=s_st, inck=16,
                 waits=[(s_s, s_s.n)])
        row2col(gc_col[:], gcq[:], D, waits=[(s_s, s_s.n)])
        for cc in range(8):
            c.do(TE, lambda e, cc=cc: e.matmul(PS_A[0:1, 0:512], gc_col[:, cc:cc + 1], gw1_s[:, ts(cc, 512)],
                                               start=(cc == 0), stop=(cc == 7)),
                 inc=(s_mm if cc == 7 else None), waits=([(s_g, s_g.n)] if cc == 0 else []))
        c.do(VE, lambda e: e.tensor_tensor(grow1[:], PS_A[0:1, 0:512], gb1_rs[:], ALU.add), inc=s_v,
             waits=[(s_mm, s_mm.n), (s_ld, nload2)])
        c.do(SC, lambda e: e.activation(out=grow1[:], in_=grow1[:], func=AF.Gelu),
             waits=[(s_v, s_v.n)])
        c.do(SC, lambda e: e.copy(out=barbuf[0:1, 6:7], in_=barbuf[0:1, 7:8]))
        c.do(SC, lambda e: e.copy(out=grow1[:], in_=grow1[:]), inc=s_s)
        if DEBUG and (DBGSEL >> 15) & 1:
            c.do(SY, lambda e: e.dma_start(out=dbg["d_grow1"], in_=grow1[:]), inc=s_st, inck=16,
                 waits=[(s_s, s_s.n)])
        row2col(gcol1[:], grow1[:], 512, waits=[(s_s, s_s.n)])
        for cc in range(4):
            c.do(TE, lambda e, cc=cc: e.matmul(PS_A[0:1, 0:256], gcol1[:, cc:cc + 1], gw2_s[:, ts(cc, 256)],
                                               start=(cc == 0), stop=(cc == 3)),
                 inc=(s_mm if cc == 3 else None), waits=([(s_g, s_g.n)] if cc == 0 else []))
        c.do(VE, lambda e: e.tensor_tensor(grow2[:], PS_A[0:1, 0:256], gb2_rs[:], ALU.add), inc=s_v,
             waits=[(s_mm, s_mm.n)])
        c.do(SC, lambda e: e.activation(out=grow2[:], in_=grow2[:], func=AF.Gelu),
             waits=[(s_v, s_v.n)])
        c.do(SC, lambda e: e.copy(out=barbuf[0:1, 6:7], in_=barbuf[0:1, 7:8]))
        c.do(SC, lambda e: e.copy(out=grow2[:], in_=grow2[:]), inc=s_s)
        if DEBUG and (DBGSEL >> 16) & 1:
            c.do(SY, lambda e: e.dma_start(out=dbg["d_grow2"], in_=grow2[:]), inc=s_st, inck=16,
                 waits=[(s_s, s_s.n)])
        row2col(gcol2[:], grow2[:], 256, waits=[(s_s, s_s.n)])
        for cc in range(2):
            c.do(TE, lambda e, cc=cc: e.matmul(PS_A[0:1, 0:1], gcol2[:, cc:cc + 1], gw3_cs[:, cc:cc + 1],
                                               start=(cc == 0), stop=(cc == 1)),
                 inc=(s_mm if cc == 1 else None), waits=([(s_g, s_g.n)] if cc == 0 else []))
        c.do(VE, lambda e: e.tensor_tensor(gf_s[:], PS_A[0:1, 0:1], gb3_ss[:], ALU.add), inc=s_v,
             waits=[(s_mm, s_mm.n)])
        c.do(SC, lambda e: e.activation(out=gf_s[:], in_=gf_s[:], func=AF.Sigmoid), inc=s_s,
             waits=[(s_v, s_v.n)])
        c.do(VE, lambda e: e.tensor_tensor(tmp1s[:], arng_s[:], gf_s[:], ALU.mult), inc=s_v,
             waits=[(s_s, s_s.n)])
        c.do(VE, lambda e: e.scalar_tensor_tensor(cA[:], tmp1s[:], 1.4, arng_s[:], ALU.mult, ALU.subtract),
             inc=s_v, waits=[(s_v, s_v.n)])
        c.do(VE, lambda e: e.tensor_scalar(cA[:], cA[:], 0.1, None, ALU.add), inc=s_v,
             waits=[(s_v, s_v.n)])
        c.do(VE, lambda e: e.tensor_scalar(cB[:], arng_s[:], 0.6, None, ALU.mult), inc=s_v)
        c.do(TE, lambda e: e.matmul(PS_B[:, 0:1], onesf[0:1, :], cA[:], start=True, stop=True),
             waits=[(s_v, s_v.n)])
        c.do(TE, lambda e: e.matmul(PS_B[:, 1:2], onesf[0:1, :], cB[:], start=True, stop=True),
             inc=s_mm)
        c.do(VE, lambda e: e.tensor_copy(cAb[:], PS_B[:, 0:1]), waits=[(s_mm, s_mm.n)])
        c.do(VE, lambda e: e.tensor_copy(cBb[:], PS_B[:, 1:2]), inc=s_v)
        c.barrier(sems["b6"])
        for dst, src_ in [(cw1_s, cw1), (cw2_s, cw2), (cw3_s, cw3)]:
            c.do(GP, lambda e, dst=dst, src_=src_: e.dma_start(
                out=dst[:], in_=src_.rearrange("(c p) j -> p c j", p=128)), inc=s_g, inck=16)
        if DEBUG and (DBGSEL >> 17) & 1:
            c.do(SY, lambda e: e.dma_start(out=dbg["d_gf"], in_=gf_s[:]), inc=s_st, inck=16)

        tf_g = {}
        for cyc in range(3):
            for q in range(4):
                for u in range(4):
                    tt = q * 4 + u
                    pst = PS_T if u % 2 == 0 else PS_T2
                    c.do(VE, lambda e, tt=tt: e.tensor_copy(sbuf_b[:], state[:, ts(tt, D)]),
                         inc=s_v, waits=[(s_tp, s_tp.n)])
                    for cc in range(8):
                        c.do(TE, lambda e, cc=cc, pst=pst: e.transpose(
                            pst[:, ts(cc, 128)], sbuf_b[:, ts(cc, 128)], id_b[:]),
                            inc=(s_tp if cc == 7 else None),
                            waits=([(s_v, s_v.n), (s_yt, max(0, s_yt.n - 1))] if cc == 0 else []))
                    for cc in range(8):
                        c.do(VE, lambda e, u=u, cc=cc, pst=pst: e.tensor_copy(
                            sT[:, cc * 512 + u * 128: cc * 512 + (u + 1) * 128],
                            pst[:, ts(cc, 128)]),
                            inc=(s_yt if cc == 7 else None),
                            waits=([(s_tp, s_tp.n)] if cc == 0 else []))
                for o in range(8):
                    pq = PS_B if o % 2 == 0 else PS_C
                    for cc in range(8):
                        c.do(TE, lambda e, o=o, cc=cc, pq=pq: e.matmul(
                            pq[:, 0:512], cw1_s[:, cc * D + o * 128: cc * D + (o + 1) * 128],
                            sT[:, ts(cc, 512)], start=(cc == 0), stop=(cc == 7)),
                            inc=(s_mm if cc == 7 else None),
                            waits=([(s_yt, s_yt.n), (s_g, s_g.n),
                                    (s_s, max(0, s_s.n - 1))] if cc == 0 else []))
                    c.do(SC, lambda e, o=o, pq=pq: e.activation(out=h1T[:, ts(o, 512)], in_=pq[:, 0:512],
                                                                func=AF.Gelu, bias=cb1_cs[:, o:o + 1],
                                                                scale=1.0),
                         inc=s_s, waits=[(s_mm, s_mm.n)])
                for o in range(4):
                    pq = PS_B if o % 2 == 0 else PS_C
                    for cc in range(8):
                        c.do(TE, lambda e, o=o, cc=cc, pq=pq: e.matmul(
                            pq[:, 0:512], cw2_s[:, cc * 512 + o * 128: cc * 512 + (o + 1) * 128],
                            h1T[:, ts(cc, 512)], start=(cc == 0), stop=(cc == 7)),
                            inc=(s_mm if cc == 7 else None),
                            waits=([(s_s, s_s.n), (s_s, max(0, s_s.n - 1))] if cc == 0 else []))
                    c.do(SC, lambda e, o=o, pq=pq: e.activation(out=h2T[:, ts(o, 512)], in_=pq[:, 0:512],
                                                                func=AF.Gelu, bias=cb2_cs[:, o:o + 1],
                                                                scale=1.0),
                         inc=s_s, waits=[(s_mm, s_mm.n)])
                for o in range(2):
                    pq = PS_B if o % 2 == 0 else PS_C
                    for cc in range(4):
                        c.do(TE, lambda e, o=o, cc=cc, pq=pq: e.matmul(
                            pq[:, 0:512], cw3_s[:, cc * 256 + o * 128: cc * 256 + (o + 1) * 128],
                            h2T[:, ts(cc, 512)], start=(cc == 0), stop=(cc == 3)),
                            inc=(s_mm if cc == 3 else None),
                            waits=([(s_s, s_s.n)] if cc == 0 else []))
                    c.do(SC, lambda e, o=o, pq=pq: e.activation(out=h3T[:, ts(o, 512)], in_=pq[:, 0:512],
                                                                func=AF.Gelu, bias=cb3_cs[:, o:o + 1],
                                                                scale=1.0),
                         inc=s_s, waits=[(s_mm, s_mm.n)])
                for cc in range(2):
                    c.do(TE, lambda e, q=q, cc=cc: e.matmul(
                        PS_A[0:1, 0:512], cw4_cs[:, cc:cc + 1], h3T[:, ts(cc, 512)],
                        start=(cc == 0), stop=(cc == 1)),
                        inc=(s_mm if cc == 1 else None),
                        waits=([(s_s, s_s.n)] if cc == 0 else []))
                c.do(VE, lambda e, q=q: e.tensor_single_scalar(tfq[:], PS_A[0:1, 0:512],
                                                               cb4_ss[0:1, 0:1], ALU.add),
                     inc=s_v, waits=[(s_mm, s_mm.n)])
                vfence()
                c.do(VE, lambda e: e.tensor_copy(tfq[:], tfq[:]), inc=s_v,
                     waits=[(s_v, s_v.n)])
                row2col(tfcol[:, q * 4: (q + 1) * 4], tfq[:], 512, waits=[(s_v, s_v.n)])
                tf_g[q] = s_g.n
            last = cyc == 2
            for q in range(4):
                c.do(SC, lambda e, q=q: e.activation(out=tfcol[:, q * 4: (q + 1) * 4],
                                                     in_=tfcol[:, q * 4: (q + 1) * 4],
                                                     func=AF.Sigmoid), inc=s_s,
                     waits=[(s_g, tf_g[q])])
                if DEBUG and cyc == 0 and q == 3:
                    c.do(SY, lambda e: e.dma_start(out=dbg["d_tf1"], in_=tfcol[:]), inc=s_st, inck=16,
                         waits=[(s_s, s_s.n)])
                c.do(VE, lambda e, q=q: e.tensor_scalar(ccol[:, q * 4: (q + 1) * 4],
                                                        tfcol[:, q * 4: (q + 1) * 4],
                                                        cBb[:, 0:1], cAb[:, 0:1],
                                                        ALU.mult, ALU.add), inc=s_v,
                     waits=[(s_s, s_s.n)])
                c.do(VE, lambda e, q=q: e.tensor_scalar(nccol[:, q * 4: (q + 1) * 4],
                                                        ccol[:, q * 4: (q + 1) * 4],
                                                        -1.0, None, ALU.mult), inc=s_v,
                     waits=[(s_v, s_v.n)])
                for t in range(q * 4, q * 4 + 4):
                    st0 = t * D
                    Q1 = state[:, st0: st0 + 256]
                    Q2 = state[:, st0 + 256: st0 + 512]
                    Q3 = state[:, st0 + 512: st0 + 768]
                    Q4 = state[:, st0 + 768: st0 + 1024]
                    c.do(VE, lambda e, Q1=Q1: e.tensor_copy(tmpq[:], Q1), inc=s_v)
                    c.do(VE, lambda e, t=t, Q1=Q1, Q3=Q3: e.scalar_tensor_tensor(
                        Q1, Q3, ccol[:, t:t + 1], Q1, ALU.mult, ALU.add), inc=s_v,
                        waits=[(s_v, s_v.n)])
                    c.do(VE, lambda e, t=t, Q3=Q3: e.scalar_tensor_tensor(
                        Q3, tmpq[:], nccol[:, t:t + 1], Q3, ALU.mult, ALU.add), inc=s_v,
                        waits=[(s_v, s_v.n)])
                    c.do(VE, lambda e, Q2=Q2: e.tensor_copy(tmpq[:], Q2), inc=s_v,
                         waits=[(s_v, s_v.n)])
                    c.do(VE, lambda e, t=t, Q2=Q2, Q4=Q4: e.scalar_tensor_tensor(
                        Q2, Q4, nccol[:, t:t + 1], Q2, ALU.mult, ALU.add), inc=s_v,
                        waits=[(s_v, s_v.n)])
                    c.do(VE, lambda e, t=t, Q4=Q4: e.scalar_tensor_tensor(
                        Q4, tmpq[:], ccol[:, t:t + 1], Q4, ALU.mult, ALU.add), inc=s_v,
                        waits=[(s_v, s_v.n)])
                    if last:
                        c.do(VE, lambda e, t=t: e.tensor_tensor(fin[:], x2[:, ts(t, D)],
                                                                state[:, ts(t, D)], ALU.add), inc=s_v,
                             waits=[(s_v, s_v.n)])
                        c.do(SY, lambda e, t=t: e.dma_start(out=out_own[ts(t, 128), :], in_=fin[:]),
                             inc=s_st, inck=16, waits=[(s_v, s_v.n)])
                        c.do(VE, lambda e: e.engine_nop(), waits=[(s_st, s_st.n)])
        if DEBUG and (DBGSEL >> 18) & 1:
            c.do(SY, lambda e: e.dma_start(out=dbg["d_state"], in_=state[:, 0:D]), inc=s_st, inck=16)

        if DEBUG and (DBGSEL >> 19) & 1:
            c.do(SY, lambda e: e.dma_start(out=dbg["d_M2"], in_=mmat[:]), inc=s_st, inck=16)
        c.do(SY, lambda e: e.nop(), waits=[(s_st, s_st.n)])
        c.barrier(sems["b9"])

    def reset_sems():
        for s in sems.values():
            s.n = 0

    def mkctx(eng, me, record=None):
        cx = Ctx(eng, me, record=record)
        cx.bar_buf = barbuf
        cx.bar_ps = PS_C[0:1, 511:512]
        cx.bar_w = onesb[0:1, 0:1]
        return cx

    if deadlock_check:
        traces = {}
        for me in ["sync", "scalar", "vector", "tensor", "gpsimd"]:
            reset_sems()
            rec = []
            prog(mkctx(None, me, record=rec))
            traces[me] = rec
        es.close()
        return traces

    with nc.Block() as block:
        @block.sync
        def _(eng):
            reset_sems()
            prog(mkctx(eng, "sync"))

        @block.scalar
        def _(eng):
            reset_sems()
            prog(mkctx(eng, "scalar"))

        @block.vector
        def _(eng):
            reset_sems()
            prog(mkctx(eng, "vector"))

        @block.tensor
        def _(eng):
            reset_sems()
            prog(mkctx(eng, "tensor"))

        @block.gpsimd
        def _(eng):
            reset_sems()
            prog(mkctx(eng, "gpsimd"))

    es.close()
    return nc


_NC_CACHE = {}


def _consts():
    e = {}
    e["ident_f"] = np.eye(128, dtype=np.float32)
    e["ident_b"] = np.eye(128).astype(ml_dtypes.bfloat16)
    e["iota2k"] = np.broadcast_to(np.arange(SK, dtype=np.float32), (128, SK)).copy()
    e["iota128r"] = np.broadcast_to(np.arange(128, dtype=np.float32), (128, 128)).copy()
    e["jconst"] = np.broadcast_to(np.arange(1, NSEL + 1, dtype=np.float32), (128, NSEL)).copy()
    e["i128c_bD"] = np.arange(128, dtype=np.float32).reshape(128, 1).astype(ml_dtypes.bfloat16)
    e["hi64_bD"] = np.broadcast_to(np.arange(NT, dtype=np.float32), (128, NT)).astype(ml_dtypes.bfloat16).copy()
    e["onesf_D"] = np.ones((128, 128), dtype=np.float32)
    e["onesb_D"] = np.ones((128, 128), dtype=ml_dtypes.bfloat16)
    return e


def kernel(L0_attn_x, L0_attn_Wq, L0_attn_bq, L0_attn_Wk, L0_attn_bk,
           L0_attn_Wv, L0_attn_bv, L0_attn_Wo, L0_attn_bo, L0_attn_sample_idx,
           L0_ffn_cn_w1, L0_ffn_cn_b1, L0_ffn_cn_w2, L0_ffn_cn_b2,
           L0_ffn_cn_w3, L0_ffn_cn_b3, L0_ffn_cn_w4, L0_ffn_cn_b4,
           L0_ffn_gc_w1, L0_ffn_gc_b1, L0_ffn_gc_w2, L0_ffn_gc_b2,
           L0_ffn_gc_w3, L0_ffn_gc_b3, L0_ffn_adaptive_range):
    x = np.asarray(L0_attn_x, dtype=np.float32)
    consts = _consts()
    if "nc" not in _NC_CACHE:
        _NC_CACHE["nc"] = build_nc()
    nc = _NC_CACHE["nc"]

    bf = ml_dtypes.bfloat16
    in_maps = []
    for c in range(8):
        b, qq = c // 4, c % 4
        hs = qq * 4 * HD
        m = dict(consts)
        m["xb"] = np.ascontiguousarray(x[b])
        m["xo"] = np.ascontiguousarray(x[b, qq * 2048:(qq + 1) * 2048])
        wq_sl = np.asarray(L0_attn_Wq, np.float32)[:, hs:hs + 256]
        wv_sl = np.asarray(L0_attn_Wv, np.float32)[:, hs:hs + 256]
        m["wvq"] = np.ascontiguousarray(np.concatenate([wv_sl, wq_sl], axis=1))
        m["wk"] = np.ascontiguousarray(np.asarray(L0_attn_Wk, np.float32)[:, hs:hs + 256])
        m["wo"] = np.ascontiguousarray(np.asarray(L0_attn_Wo, np.float32))
        bq = np.asarray(L0_attn_bq, np.float32)[hs:hs + 256]
        bk = np.asarray(L0_attn_bk, np.float32)[hs:hs + 256]
        bv = np.asarray(L0_attn_bv, np.float32)[hs:hs + 256]
        m["bvq_rbD"] = np.concatenate([bv, bq]).reshape(1, 512).astype(bf)
        m["bk_c"] = np.ascontiguousarray(bk.reshape(2, 128).T)
        m["bk_r"] = bk.reshape(1, 256)
        m["bo_r"] = np.asarray(L0_attn_bo, np.float32).reshape(1, D)
        m["sidx"] = np.ascontiguousarray(np.asarray(L0_attn_sample_idx, np.int32).reshape(16, 128).T)
        m["cw1"] = np.asarray(L0_ffn_cn_w1, np.float32)
        m["cw2"] = np.asarray(L0_ffn_cn_w2, np.float32)
        m["cw3"] = np.asarray(L0_ffn_cn_w3, np.float32)
        m["cw4_c"] = np.ascontiguousarray(np.asarray(L0_ffn_cn_w4, np.float32).reshape(2, 128).T)
        m["cb1_c"] = np.ascontiguousarray(np.asarray(L0_ffn_cn_b1, np.float32).reshape(8, 128).T)
        m["cb2_c"] = np.ascontiguousarray(np.asarray(L0_ffn_cn_b2, np.float32).reshape(4, 128).T)
        m["cb3_c"] = np.ascontiguousarray(np.asarray(L0_ffn_cn_b3, np.float32).reshape(2, 128).T)
        m["cb4_s"] = np.asarray(L0_ffn_cn_b4, np.float32).reshape(1, 1)
        m["gw1"] = np.asarray(L0_ffn_gc_w1, np.float32)
        m["gw2"] = np.asarray(L0_ffn_gc_w2, np.float32)
        m["gw3_c"] = np.ascontiguousarray(np.asarray(L0_ffn_gc_w3, np.float32).reshape(2, 128).T)
        m["gb1_r"] = np.asarray(L0_ffn_gc_b1, np.float32).reshape(1, 512)
        m["gb2_r"] = np.asarray(L0_ffn_gc_b2, np.float32).reshape(1, 256)
        m["gb3_s"] = np.asarray(L0_ffn_gc_b3, np.float32).reshape(1, 1)
        m["arng"] = np.asarray(L0_ffn_adaptive_range, np.float32).reshape(1, 1)
        m["tok0"] = np.asarray([[qq * 2048]], np.float32)
        in_maps.append(m)

    res = run_bass_kernel_spmd(nc, in_maps, core_ids=list(range(8)))
    kernel._last_res = res
    out = np.empty((B, S, D), dtype=np.float32)
    for c in range(8):
        b, qq = c // 4, c % 4
        out[b, qq * 2048:(qq + 1) * 2048] = res.results[c]["out_own"]
    return out
